# version 01
# baseline (speedup 1.0000x reference)
import numpy as np
import ml_dtypes
import concourse.bass as bass
import concourse.mybir as mybir
from concourse.bass_utils import run_bass_kernel_spmd

F32 = mybir.dt.float32
BF16 = mybir.dt.bfloat16
FP8 = mybir.dt.float8e4
AF = mybir.ActivationFunctionType
ALU = mybir.AluOpType
AX = mybir.AxisListType

S = 2048
D = 2048
NH = 32
HD = 64
ROT = 32
NPAIR = 16
NC = 16
bfdt = ml_dtypes.bfloat16
f8dt = ml_dtypes.float8_e4m3fn


def _host_consts():
    t = np.arange(S, dtype=np.float64)
    inv = 10000.0 ** (-np.arange(0, ROT, 2, dtype=np.float64) / ROT)
    fr = np.outer(t, inv)
    cos = np.cos(fr)
    sin = np.sin(fr)
    ctab = np.zeros((128, S), dtype=np.float64)
    stab = np.zeros((128, S), dtype=np.float64)
    for r in range(128):
        hb = r % 64
        if hb < 16:
            ctab[r] = cos[:, hb]
            stab[r] = sin[:, hb]
        elif hb < 32:
            ctab[r] = cos[:, hb - 16]
            stab[r] = -sin[:, hb - 16]
        else:
            ctab[r] = 1.0
            stab[r] = 0.0
    tri = (np.arange(128)[None, :] >= np.arange(128)[:, None])
    return (ctab.astype(bfdt), stab.astype(bfdt), tri.astype(bfdt),
            np.eye(128, dtype=np.float32), np.eye(128, dtype=bfdt),
            np.eye(128, dtype=f8dt))


class Prog:
    ENGS = ("sync", "gpsimd", "tensor", "vector", "scalar")

    def __init__(self, nc, es):
        self.nc = nc
        self.ops = {e: [] for e in self.ENGS}
        self.sem = {}
        self.cnt = {}
        self.satisfied = {e: {} for e in self.ENGS}
        for e in self.ENGS:
            self.sem[e] = es.enter_context(nc.semaphore("sem_" + e))
            self.cnt[e] = 0

    def _wait(self, eng, prod, val):
        if val <= 0:
            return
        if self.satisfied[eng].get(prod, 0) >= val:
            return
        self.satisfied[eng][prod] = val
        sem = self.sem[prod]
        self.ops[eng].append(lambda e, s=sem, v=val: e.wait_ge(s, v))

    def op(self, eng, fn, reads=(), writes=(), inc=1, small=False):
        deps = {}
        need_drain = False
        for r in reads:
            for p, v in r.w.items():
                deps[p] = max(deps.get(p, 0), v)
            if r.w_eng == eng and (r.w_small or small):
                need_drain = True
        for r in writes:
            for p, v in r.w.items():
                deps[p] = max(deps.get(p, 0), v)
            for p, v in r.r.items():
                deps[p] = max(deps.get(p, 0), v)
        for p, v in deps.items():
            if p != eng:
                self._wait(eng, p, v)
        if need_drain:
            self.ops[eng].append(lambda e: e.drain())
        self.cnt[eng] += inc
        n = self.cnt[eng]
        sem = self.sem[eng]
        if inc:
            self.ops[eng].append(lambda e, f=fn, s=sem, a=inc: f(e).then_inc(s, a))
        else:
            self.ops[eng].append(fn)
        for r in writes:
            r.w = {eng: n}
            r.r = {}
            r.w_eng = eng
            r.w_small = small
        for r in reads:
            r.r[eng] = n
        return n

    def barrier(self):
        snap = dict(self.cnt)
        for e in self.ENGS:
            for p, v in snap.items():
                if p != e and v > 0:
                    self._wait(e, p, v)


class Res:
    __slots__ = ("w", "r", "w_eng", "w_small")

    def __init__(self):
        self.w = {}
        self.r = {}
        self.w_eng = None
        self.w_small = False


def build_nc(qdq_mul=4.0 / 3.0, ffn_scale=1.125, debug=False):
    nc = bass.Bass("TRN2", target_bir_lowering=False)
    from contextlib import ExitStack
    es = ExitStack()

    dX = nc.dram_tensor("x", [S, D], F32, kind="ExternalInput")
    dWqkv = nc.dram_tensor("wqkv", [D, 3 * D], F32, kind="ExternalInput")
    dWout = nc.dram_tensor("wout", [D, D], F32, kind="ExternalInput")
    dW8 = nc.dram_tensor("w8", [D, D], FP8, kind="ExternalInput")
    dBar = nc.dram_tensor("barena", [128, D], F32, kind="ExternalInput")
    dCtab = nc.dram_tensor("ctab", [128, S], BF16, kind="ExternalInput")
    dStab = nc.dram_tensor("stab", [128, S], BF16, kind="ExternalInput")
    dTri = nc.dram_tensor("tri", [128, 128], BF16, kind="ExternalInput")
    dIf = nc.dram_tensor("identf", [128, 128], F32, kind="ExternalInput")
    dIb = nc.dram_tensor("identb", [128, 128], BF16, kind="ExternalInput")
    dI8 = nc.dram_tensor("ident8", [128, 128], FP8, kind="ExternalInput")
    dY = nc.dram_tensor("y", [S, D], F32, kind="ExternalOutput")

    ps = lambda n, sh, dt: es.enter_context(nc.psum_tensor(n, sh, dt))

    ARENA = 207 * 1024
    arena = es.enter_context(nc.sbuf_tensor("sb_arena", [128, ARENA], mybir.dt.uint8))
    base = nc.lookup_mloc(arena).addr
    cur = [0]

    def _sz(sh, dt):
        n = 1
        for d in sh[1:]:
            n *= d
        return (n * mybir.dt.size(dt) + 127) & ~127

    def sb(name, sh, dt, at=None):
        off = cur[0] if at is None else at
        if at is None:
            cur[0] += _sz(sh, dt)
            assert cur[0] <= ARENA, (name, cur[0])
        return nc.alloc_sbuf_tensor_at(name, sh, dt, offset=base + off)

    _x0 = cur[0]
    xnT = sb("xnT", [128, NC * S], BF16)
    w8T = sb("w8T", [128, NC * D], FP8, at=_x0)
    WoB = sb("WoB", [128, 2 * NC * 512], BF16, at=_x0 + NC * D)

    _y0 = cur[0]
    qT = sb("qT", [128, S], BF16)
    kT = sb("kT", [128, S], BF16)
    vA = sb("vA", [128, NC * HD], BF16)
    vB = sb("vB", [128, NC * HD], BF16)
    Wq = sb("Wq", [128, NC * 128], BF16)
    Wk = sb("Wk", [128, NC * 128], BF16)
    Wv = sb("Wv", [128, NC * 128], BF16)
    probs = sb("probs", [128, 4 * 512], BF16)
    ctab = sb("ctab", [128, S], BF16)
    stab = sb("stab", [128, S], BF16)
    xq8T = sb("xq8T", [128, NC * S], FP8, at=_y0)
    xq8 = sb("xq8", [128, D], FP8, at=_y0 + NC * S)

    aT = sb("aT", [128, NC * S], BF16)
    Wst = sb("Wst", [128, 2048], F32)
    tri = sb("tri", [128, 128], BF16)
    idf = sb("idf", [128, 128], F32)
    idb = sb("idb", [128, 128], BF16)
    id8 = sb("id8", [128, 128], FP8)
    barena = sb("barena", [128, D], F32)
    onesF = sb("onesF", [128, 512], F32)
    onesB = sb("onesB", [128, 16], BF16)
    xt = sb("xt", [128, 2048], F32)
    scrF = sb("scrF", [128, 2048], F32)
    sRec = sb("sRec", [128, 512], F32)
    miscF = sb("miscF", [128, 16], F32)
    negc = sb("negc", [128, 1], F32)
    lnT = sb("lnT", [128, 12], F32)
    epsT = sb("epsT", [128, 1], F32)

    pb = [ps(f"pb{i}", [128, 512], F32) for i in range(7)]
    p8 = ps("p8", [128, 2048], FP8)

    P = Prog(nc, es)

    R = lambda: Res()
    rXt, rWst, rWq, rWk, rWv = R(), R(), R(), R(), R()
    rQT, rKT, rVA, rVB = R(), R(), R(), R()
    rXnT, rAT = R(), R()
    rPb = [R() for _ in range(7)]
    rP8 = R()
    rProbs = [R() for _ in range(4)]
    rScrF = R()
    rSRec, rMisc, rNegc = R(), R(), R()
    rLn = R()
    rW8T, rXq8, rXq8T, rWoB = R(), R(), R(), [R(), R()]
    rConst = R()
    rY = R()

    def dma(eng, dst, src):
        return lambda e, d=dst, s=src: e.dma_start(d, s)

    for dst, src in ((ctab[:, :], dCtab.ap()), (stab[:, :], dStab.ap()),
                     (tri[:, :], dTri.ap()), (idf[:, :], dIf.ap()),
                     (idb[:, :], dIb.ap()), (id8[:, :], dI8.ap()),
                     (barena[:, :], dBar.ap())):
        P.op("sync", dma("sync", dst, src), writes=(rConst,), inc=16)
    P.op("vector", lambda e: e.memset(onesF[:, :], 1.0), writes=(rConst,))
    P.op("vector", lambda e: e.memset(onesB[:, :], 1.0), writes=(rConst,))
    P.op("vector", lambda e: e.memset(epsT[:, :], 1e-5), writes=(rConst,))

    def emit_ln():
        P.op("vector", lambda e: e.reduce_sum(lnT[:, 0:1], xt[:, :], AX.X),
             reads=(rXt,), writes=(rLn,), small=True)
        P.op("scalar", lambda e: e.mul(lnT[:, 1:2], lnT[:, 0:1], 1.0 / D),
             reads=(rLn,), writes=(rLn,), small=True)
        P.op("scalar", lambda e: e.square(scrF[:, 0:2048], xt[:, :]),
             reads=(rXt,), writes=(rScrF,))
        P.op("vector", lambda e: e.reduce_sum(lnT[:, 2:3], scrF[:, 0:2048], AX.X),
             reads=(rScrF,), writes=(rLn,), small=True)
        P.op("scalar", lambda e: e.mul(lnT[:, 3:4], lnT[:, 2:3], 1.0 / D),
             reads=(rLn,), writes=(rLn,), small=True)
        P.op("vector", lambda e: e.tensor_tensor(lnT[:, 4:5], lnT[:, 1:2],
                                                 lnT[:, 1:2], ALU.mult),
             reads=(rLn,), writes=(rLn,), small=True)
        P.op("vector", lambda e: e.tensor_tensor(lnT[:, 5:6], lnT[:, 3:4],
                                                 lnT[:, 4:5], ALU.subtract),
             reads=(rLn,), writes=(rLn,), small=True)
        P.op("vector", lambda e: e.tensor_scalar(lnT[:, 6:7], lnT[:, 5:6],
                                                 epsT[:, 0:1], None, ALU.add),
             reads=(rConst, rLn), writes=(rLn,), small=True)
        P.op("vector", lambda e: e.reciprocal(lnT[:, 7:8], lnT[:, 6:7]),
             reads=(rLn,), writes=(rLn,), small=True)
        P.op("scalar", lambda e: e.sqrt(lnT[:, 8:9], lnT[:, 7:8]),
             reads=(rLn,), writes=(rLn,), small=True)

    for tc in range(NC):
        P.op("sync", dma("sync", xt[:, :],
                         bass.AP(dX, tc * 128 * D, [[D, 128], [1, D]])),
             writes=(rXt,), inc=16)
        emit_ln()
        P.op("vector", lambda e: e.tensor_scalar(xt[:, :], xt[:, :], lnT[:, 1:2],
                                                 lnT[:, 8:9], ALU.subtract, ALU.mult),
             reads=(rLn,), writes=(rXt,))
        for i in range(NC):
            b = i // 4
            P.op("tensor", lambda e, i=i, b=b: e.transpose(
                pb[b][:, 128 * (i % 4):128 * (i % 4) + 128], xt[:, 128 * i:128 * i + 128],
                idf[:, :]),
                reads=(rXt, rConst), writes=(rPb[b],))
        for b in range(4):
            P.op("vector", lambda e, b=b, tc=tc: e.tensor_copy(
                bass.AP(xnT, 8192 * b + 128 * tc, [[NC * S, 128], [2048, 4], [1, 128]]),
                pb[b][:, :]),
                reads=(rPb[b],), writes=(rXnT,))

    dbg_out = {}

    def dbg(name, src_ap, shape, dt, res):
        if not debug:
            return
        d = nc.dram_tensor("dbg_" + name, shape, dt, kind="ExternalOutput")
        dbg_out[name] = d
        P.op("gpsimd", lambda e: e.dma_start(d.ap(), src_ap), reads=(res,), inc=16)

    dbg("xnT", xnT[:, :], [128, NC * S], BF16, rXnT)

    for p in range(NPAIR):
        for bi, (c0, rW) in enumerate(((128 * p, rWq), (D + 128 * p, rWk),
                                       (2 * D + 128 * p, rWv))):
            P.op("sync", dma("sync", bass.AP(Wst, 0, [[2048, 128], [128, 16], [1, 128]]),
                             bass.AP(dWqkv, c0, [[3 * D, 128], [128 * 3 * D, 16], [1, 128]])),
                 writes=(rWst,), inc=16)
            dstW = (Wq, Wk, Wv)[bi]
            P.op("vector", lambda e, d=dstW: e.tensor_copy(d[:, :], Wst[:, :]),
                 reads=(rWst,), writes=(rW,))

        for mi, (Wm, rWm, brow, dstT, rDst) in enumerate(
                ((Wq, rWq, 0, qT, rQT), (Wk, rWk, 32, kT, rKT))):
            for tc4 in range(4):
                bank = 4 + (tc4 % 2)
                P.op("tensor", lambda e, brow=brow, p=p, bank=bank: e.matmul(
                    pb[bank][:, :], barena[brow:brow + 1, 128 * p:128 * p + 128],
                    onesF[brow:brow + 1, 0:512], start=True, stop=False),
                    reads=(rConst,), writes=(rPb[bank],))
                for dc in range(NC):
                    P.op("tensor", lambda e, Wm=Wm, dc=dc, tc4=tc4, bank=bank: e.matmul(
                        pb[bank][:, :], Wm[:, 128 * dc:128 * dc + 128],
                        xnT[:, 2048 * dc + 512 * tc4:2048 * dc + 512 * tc4 + 512],
                        start=False, stop=(dc == NC - 1)),
                        reads=(rWm, rXnT), writes=(rPb[bank],))
                cs = slice(512 * tc4, 512 * tc4 + 512)
                P.op("vector", lambda e, bank=bank, cs=cs: e.tensor_tensor(
                    scrF[:, 0:512], pb[bank][:, :], ctab[:, cs], ALU.mult),
                    reads=(rPb[bank], rConst), writes=(rScrF,))
                P.op("vector", lambda e, bank=bank, cs=cs: e.tensor_tensor(
                    scrF[:, 512:1024], pb[bank][:, :], stab[:, cs], ALU.mult),
                    reads=(rPb[bank], rConst), writes=(rScrF,))
                P.op("vector", lambda e: e.stream_shuffle(
                    scrF[:, 1024:1536], scrF[:, 512:1024],
                    [(i + 16) % 32 for i in range(32)]),
                    reads=(rScrF,), writes=(rScrF,))
                P.op("vector", lambda e, dstT=dstT, cs=cs: e.tensor_tensor(
                    dstT[:, cs], scrF[:, 0:512], scrF[:, 1024:1536], ALU.add),
                    reads=(rScrF,), writes=(rDst,))

        P.op("vector", lambda e: e.memset(sRec[64:65, :], 0.0), writes=(rSRec,), small=True)
        P.op("scalar", lambda e: e.square(probs[:, 0:2048], qT[:, :]),
             reads=(rQT,), writes=(rProbs[0], rProbs[1], rProbs[2], rProbs[3]))
        for i in range(4):
            P.op("tensor", lambda e, i=i: e.matmul(
                pb[i][0:1, :], onesB[:, 0:1], probs[:, 512 * i:512 * i + 512],
                start=True, stop=True),
                reads=(rProbs[i], rConst), writes=(rPb[i],))
        for i in range(4):
            P.op("vector", lambda e, i=i: e.tensor_tensor(
                sRec[64:65, :], sRec[64:65, :], pb[i][0:1, :], ALU.max),
                reads=(rPb[i], rSRec), writes=(rSRec,), small=True)
        P.op("scalar", lambda e: e.square(probs[:, 0:2048], kT[:, :]),
             reads=(rKT,), writes=(rProbs[0], rProbs[1], rProbs[2], rProbs[3]))
        for i in range(4):
            P.op("tensor", lambda e, i=i: e.matmul(
                pb[i][0:1, :], onesB[:, 0:1], probs[:, 512 * i:512 * i + 512],
                start=True, stop=True),
                reads=(rProbs[i], rConst), writes=(rPb[i],))
        for i in range(4):
            P.op("vector", lambda e, i=i: e.tensor_tensor(
                sRec[64:65, :], sRec[64:65, :], pb[i][0:1, :], ALU.max),
                reads=(rPb[i], rSRec), writes=(rSRec,), small=True)
        P.op("vector", lambda e: e.reduce_max(miscF[64:65, 0:1], sRec[64:65, :], AX.X),
             reads=(rSRec,), writes=(rMisc,), small=True)
        P.op("scalar", lambda e: e.mul(miscF[64:65, 1:2], miscF[64:65, 0:1], -0.125),
             reads=(rMisc,), writes=(rMisc,), small=True)
        P.op("tensor", lambda e: e.matmul(pb[6][:, 0:1], onesF[64:65, 0:128],
                                          miscF[64:65, 1:2], start=True, stop=True),
             reads=(rMisc, rConst), writes=(rPb[6],))
        P.op("vector", lambda e: e.tensor_copy(negc[:, :], pb[6][:, 0:1]),
             reads=(rPb[6],), writes=(rNegc,), small=True)

        for tc4 in range(4):
            bank = 4 + (tc4 % 2)
            P.op("tensor", lambda e, p=p, bank=bank: e.matmul(
                pb[bank][:, :], barena[64:65, 128 * p:128 * p + 128],
                onesF[64:65, 0:512], start=True, stop=False),
                reads=(rConst,), writes=(rPb[bank],))
            for dc in range(NC):
                P.op("tensor", lambda e, dc=dc, tc4=tc4, bank=bank: e.matmul(
                    pb[bank][:, :], Wv[:, 128 * dc:128 * dc + 128],
                    xnT[:, 2048 * dc + 512 * tc4:2048 * dc + 512 * tc4 + 512],
                    start=False, stop=(dc == NC - 1)),
                    reads=(rWv, rXnT), writes=(rPb[bank],))
            P.op("vector", lambda e, bank=bank: e.tensor_copy(scrF[:, 0:512], pb[bank][:, :]),
                 reads=(rPb[bank],), writes=(rScrF,))
            for i in range(4):
                P.op("tensor", lambda e, i=i: e.transpose(
                    pb[6][:, 128 * i:128 * i + 128], scrF[:, 128 * i:128 * i + 128],
                    idf[:, :]),
                    reads=(rScrF, rConst), writes=(rPb[6],))
            for i in range(4):
                sc = tc4 * 4 + i
                P.op("vector", lambda e, i=i, sc=sc: e.tensor_copy(
                    vA[:, 64 * sc:64 * sc + 64], pb[6][:, 128 * i:128 * i + 64]),
                    reads=(rPb[6],), writes=(rVA,))
                P.op("vector", lambda e, i=i, sc=sc: e.tensor_copy(
                    vB[:, 64 * sc:64 * sc + 64], pb[6][:, 128 * i + 64:128 * i + 128]),
                    reads=(rPb[6],), writes=(rVB,))

        if p == 0:
            dbg("qT0", qT[:, :], [128, S], BF16, rQT)
            dbg("kT0", kT[:, :], [128, S], BF16, rKT)
            dbg("negc0", negc[:, :], [128, 1], F32, rNegc)
            dbg("vA0", vA[:, :], [128, NC * HD], BF16, rVA)
            dbg("vB0", vB[:, :], [128, NC * HD], BF16, rVB)

        for T in range(4):
            jlist = [4 * T] + list(range(0, 4 * T)) + [4 * T + r for r in (1, 2, 3)]
            for idx, j in enumerate(jlist):
                par = idx % 2
                bA, bB = par, 2 + par
                r0 = max(0, 128 * (j - 4 * T))
                cs = slice(r0, 512)
                first = (idx == 0)
                P.op("tensor", lambda e, j=j, T=T, cs=cs, bA=bA: e.matmul(
                    pb[bA][:, cs], kT[0:64, 128 * j:128 * j + 128],
                    qT[0:64, 512 * T + cs.start:512 * T + 512],
                    start=True, stop=True, tile_position=(0, 0)),
                    reads=(rKT, rQT), writes=(rPb[bA],))
                P.op("tensor", lambda e, j=j, T=T, cs=cs, bB=bB: e.matmul(
                    pb[bB][:, cs], kT[64:128, 128 * j:128 * j + 128],
                    qT[64:128, 512 * T + cs.start:512 * T + 512],
                    start=True, stop=True, tile_position=(64, 0)),
                    reads=(rKT, rQT), writes=(rPb[bB],))
                pApr, pBpr = rProbs[par], rProbs[2 + par]
                P.op("scalar", lambda e, cs=cs, bA=bA, par=par: e.activation(
                    probs[:, 512 * par + cs.start:512 * par + 512], pb[bA][:, cs],
                    AF.Exp, bias=negc[:, 0:1], scale=0.125),
                    reads=(rPb[bA], rNegc), writes=(pApr,))
                P.op("scalar", lambda e, cs=cs, bB=bB, par=par: e.activation(
                    probs[:, 1024 + 512 * par + cs.start:1024 + 512 * par + 512],
                    pb[bB][:, cs], AF.Exp, bias=negc[:, 0:1], scale=0.125),
                    reads=(rPb[bB], rNegc), writes=(pBpr,))
                if j >= 4 * T:
                    m0 = r0
                    P.op("vector", lambda e, m0=m0, par=par: e.tensor_tensor(
                        probs[:, 512 * par + m0:512 * par + m0 + 128],
                        probs[:, 512 * par + m0:512 * par + m0 + 128],
                        tri[:, :], ALU.mult),
                        reads=(rConst,), writes=(pApr,))
                    P.op("vector", lambda e, m0=m0, par=par: e.tensor_tensor(
                        probs[:, 1024 + 512 * par + m0:1024 + 512 * par + m0 + 128],
                        probs[:, 1024 + 512 * par + m0:1024 + 512 * par + m0 + 128],
                        tri[:, :], ALU.mult),
                        reads=(rConst,), writes=(pBpr,))
                P.op("tensor", lambda e, cs=cs, par=par, first=first: e.matmul(
                    pb[5][0:1, cs], onesB[:, 0:1],
                    probs[:, 512 * par + cs.start:512 * par + 512],
                    start=first, stop=False, tile_position=(0, 0)),
                    reads=(pApr, rConst), writes=(rPb[5],))
                P.op("tensor", lambda e, cs=cs, par=par, first=first: e.matmul(
                    pb[5][32:33, cs], onesB[:, 0:1],
                    probs[:, 1024 + 512 * par + cs.start:1024 + 512 * par + 512],
                    start=first, stop=False, tile_position=(0, 32)),
                    reads=(pBpr, rConst), writes=(rPb[5],))
                P.op("tensor", lambda e, j=j, cs=cs, par=par, first=first: e.matmul(
                    pb[4][0:64, cs], vA[:, 64 * j:64 * j + 64],
                    probs[:, 512 * par + cs.start:512 * par + 512],
                    start=first, stop=False, tile_position=(0, 0)),
                    reads=(pApr, rVA), writes=(rPb[4],))
                P.op("tensor", lambda e, j=j, cs=cs, par=par, first=first: e.matmul(
                    pb[4][64:128, cs], vB[:, 64 * j:64 * j + 64],
                    probs[:, 1024 + 512 * par + cs.start:1024 + 512 * par + 512],
                    start=first, stop=False, tile_position=(0, 64)),
                    reads=(pBpr, rVB), writes=(rPb[4],))
            P.op("vector", lambda e: e.reciprocal(sRec[0:1, :], pb[5][0:1, :]),
                 reads=(rPb[5],), writes=(rSRec,), small=True)
            P.op("vector", lambda e: e.reciprocal(sRec[32:33, :], pb[5][32:33, :]),
                 reads=(rPb[5],), writes=(rSRec,), small=True)
            P.op("tensor", lambda e: e.matmul(pb[6][0:64, :], onesF[0:1, 0:64],
                                              sRec[0:1, :], start=True, stop=True,
                                              tile_position=(0, 0)),
                 reads=(rSRec, rConst), writes=(rPb[6],))
            P.op("tensor", lambda e: e.matmul(pb[6][64:128, :], onesF[32:33, 0:64],
                                              sRec[32:33, :], start=True, stop=True,
                                              tile_position=(32, 64)),
                 reads=(rSRec, rConst), writes=(rPb[6],))
            P.op("vector", lambda e: e.tensor_copy(scrF[:, 0:512], pb[6][:, :]),
                 reads=(rPb[6],), writes=(rScrF,))
            P.op("vector", lambda e, p=p, T=T: e.tensor_tensor(
                aT[0:64, 2048 * p + 512 * T:2048 * p + 512 * T + 512],
                pb[4][0:64, :], scrF[0:64, 0:512], ALU.mult),
                reads=(rPb[4], rScrF), writes=(rAT,))
            P.op("vector", lambda e, p=p, T=T: e.tensor_tensor(
                aT[64:128, 2048 * p + 512 * T:2048 * p + 512 * T + 512],
                pb[4][64:128, :], scrF[64:128, 0:512], ALU.mult),
                reads=(rPb[4], rScrF), writes=(rAT,))

    dbg("aT", aT[:, :], [128, NC * S], BF16, rAT)

    P.barrier()

    for oc8 in range(NC):
        P.op("gpsimd", dma("gpsimd", xq8[:, :],
                           bass.AP(dW8, oc8 * 128 * D, [[D, 128], [1, D]])),
             writes=(rXq8,), inc=16)
        for g in range(2):
            for i in range(8):
                ic = 8 * g + i
                P.op("tensor", lambda e, ic=ic, i=i: e.transpose(
                    bass.AP(p8, 256 * i, [[2048, 128], [2, 128]]),
                    xq8[:, 128 * ic:128 * ic + 128], id8[:, :]),
                    reads=(rXq8, rConst), writes=(rP8,))
            P.op("vector", lambda e, g=g, oc8=oc8: e.tensor_copy(
                bass.AP(w8T, 2048 * 8 * g + 128 * oc8, [[NC * D, 128], [2048, 8], [1, 128]]),
                bass.AP(p8, 0, [[2048, 128], [256, 8], [2, 128]])),
                reads=(rP8,), writes=(rW8T,))

    for tc in range(NC):
        P.op("sync", dma("sync", xt[:, :],
                         bass.AP(dX, tc * 128 * D, [[D, 128], [1, D]])),
             writes=(rXt,), inc=16)
        emit_ln()
        P.op("scalar", lambda e: e.mul(lnT[:, 9:10], lnT[:, 8:9], qdq_mul),
             reads=(rLn,), writes=(rLn,), small=True)
        P.op("vector", lambda e: e.tensor_scalar(xq8[:, :], xt[:, :], lnT[:, 1:2],
                                                 lnT[:, 9:10], ALU.subtract, ALU.mult),
             reads=(rXt, rLn), writes=(rXq8,))
        for g in range(2):
            for i in range(8):
                ic = 8 * g + i
                P.op("tensor", lambda e, ic=ic, i=i: e.transpose(
                    bass.AP(p8, 256 * i, [[2048, 128], [2, 128]]),
                    xq8[:, 128 * ic:128 * ic + 128], id8[:, :]),
                    reads=(rXq8, rConst), writes=(rP8,))
            P.op("vector", lambda e, g=g, tc=tc: e.tensor_copy(
                bass.AP(xq8T, 2048 * 8 * g + 128 * tc, [[NC * S, 128], [2048, 8], [1, 128]]),
                bass.AP(p8, 0, [[2048, 128], [256, 8], [2, 128]])),
                reads=(rP8,), writes=(rXq8T,))

    for oc in range(4):
        pg = oc % 2
        for sub in range(4):
            o0 = 512 * oc + 128 * sub
            P.op("sync", dma("sync", bass.AP(Wst, 0, [[2048, 128], [128, 16], [1, 128]]),
                             bass.AP(dWout, o0, [[D, 128], [128 * D, 16], [1, 128]])),
                 writes=(rWst,), inc=16)
            P.op("vector", lambda e, pg=pg, sub=sub: e.tensor_copy(
                bass.AP(WoB, NC * 512 * pg + 128 * sub, [[2 * NC * 512, 128], [512, 16], [1, 128]]),
                Wst[:, :]),
                reads=(rWst,), writes=(rWoB[pg],))
        for tc in range(NC):
            pby = tc % 2
            pbf = 2 + (tc % 2)
            P.op("gpsimd", dma("gpsimd", scrF[:, 512 * (tc % 2):512 * (tc % 2) + 512],
                               bass.AP(dX, tc * 128 * D + 512 * oc, [[D, 128], [1, 512]])),
                 writes=(rScrF,), inc=16)
            P.op("tensor", lambda e, oc=oc, pby=pby: e.matmul(
                pb[pby][:, :], onesF[96:97, 0:128], barena[96:97, 512 * oc:512 * oc + 512],
                start=True, stop=False, tile_position=(96, 0)),
                reads=(rConst,), writes=(rPb[pby],))
            for dc in range(NC):
                P.op("tensor", lambda e, dc=dc, tc=tc, pg=pg, pby=pby: e.matmul(
                    pb[pby][:, :], aT[:, 2048 * dc + 128 * tc:2048 * dc + 128 * tc + 128],
                    WoB[:, NC * 512 * pg + 512 * dc:NC * 512 * pg + 512 * dc + 512],
                    start=False, stop=(dc == NC - 1)),
                    reads=(rAT, rWoB[pg]), writes=(rPb[pby],))
            for ic in range(NC):
                P.op("tensor", lambda e, ic=ic, tc=tc, oc=oc, pbf=pbf: e.matmul(
                    pb[pbf][:, :], xq8T[:, 2048 * ic + 128 * tc:2048 * ic + 128 * tc + 128],
                    w8T[:, 2048 * ic + 512 * oc:2048 * ic + 512 * oc + 512],
                    start=(ic == 0), stop=(ic == NC - 1)),
                    reads=(rXq8T, rW8T), writes=(rPb[pbf],))
            ts = 1024 + 512 * (tc % 2)
            P.op("vector", lambda e, tc=tc, pbf=pbf, ts=ts: e.scalar_tensor_tensor(
                scrF[:, ts:ts + 512], pb[pbf][:, :], ffn_scale,
                scrF[:, 512 * (tc % 2):512 * (tc % 2) + 512],
                ALU.mult, ALU.add),
                reads=(rPb[pbf],), writes=(rScrF,))
            P.op("vector", lambda e, pby=pby, ts=ts: e.tensor_tensor(
                scrF[:, ts:ts + 512], scrF[:, ts:ts + 512],
                pb[pby][:, :], ALU.add),
                reads=(rPb[pby],), writes=(rScrF,))
            P.op("gpsimd", lambda e, tc=tc, oc=oc, ts=ts: e.dma_start(
                bass.AP(dY, tc * 128 * D + 512 * oc, [[D, 128], [1, 512]]),
                scrF[:, ts:ts + 512]),
                reads=(rScrF,), writes=(rY,), inc=16)

    dbg("xq8T", xq8T[:, :], [128, NC * S], FP8, rXq8T)
    dbg("w8T", w8T[:, :], [128, NC * D], FP8, rW8T)

    P._wait("gpsimd", "gpsimd", P.cnt["gpsimd"])

    with nc.Block() as block:
        @block.sync
        def _(e):
            for f in P.ops["sync"]:
                f(e)

        @block.gpsimd
        def _(e):
            for f in P.ops["gpsimd"]:
                f(e)

        @block.tensor
        def _(e):
            for f in P.ops["tensor"]:
                f(e)

        @block.vector
        def _(e):
            for f in P.ops["vector"]:
                f(e)

        @block.scalar
        def _(e):
            for f in P.ops["scalar"]:
                f(e)

    return nc, es


def kernel(L0_attn_x, L0_attn_Wqkv, L0_attn_b_qkv, L0_attn_Wout, L0_attn_b_out,
           L0_ffn_weight, L0_ffn_weight_scale, L0_ffn_input_scale):
    x = np.ascontiguousarray(np.asarray(L0_attn_x), dtype=np.float32)
    wqkv = np.ascontiguousarray(np.asarray(L0_attn_Wqkv), dtype=np.float32)
    wout = np.ascontiguousarray(np.asarray(L0_attn_Wout), dtype=np.float32)
    w8 = np.asarray(L0_ffn_weight)
    if w8.dtype != f8dt:
        w8 = w8.view(f8dt) if w8.dtype.itemsize == 1 else w8.astype(f8dt)
    w8 = np.ascontiguousarray(w8)
    bq = np.asarray(L0_attn_b_qkv, dtype=np.float32)
    bo = np.asarray(L0_attn_b_out, dtype=np.float32)

    barena = np.zeros((128, D), dtype=np.float32)
    barena[0] = bq[0:D]
    barena[32] = bq[D:2 * D]
    barena[64] = bq[2 * D:3 * D]
    barena[96] = bo

    ctab, stab, tri, idf, idb, id8 = _host_consts()

    s_in = float(np.asarray(L0_ffn_input_scale, dtype=np.float32))
    s_w = float(np.asarray(L0_ffn_weight_scale, dtype=np.float32))
    nc, es = build_nc(qdq_mul=float(np.float32(1.0) / np.float32(s_in)),
                      ffn_scale=float(np.float32(s_w) * np.float32(s_in)))
    shared = {
        "wqkv": wqkv, "wout": wout, "w8": w8, "barena": barena,
        "ctab": ctab, "stab": stab, "tri": tri,
        "identf": idf, "identb": idb, "ident8": id8,
    }
    in_maps = [dict(shared, x=np.ascontiguousarray(x[c])) for c in range(8)]
    res = run_bass_kernel_spmd(nc, in_maps, core_ids=list(range(8)))
    es.close()
    out = np.stack([np.asarray(r["y"], dtype=np.float32) for r in res.results], axis=0)
    return out


# version 02
# speedup vs baseline: 1.2144x; 1.2144x over previous
import numpy as np
import ml_dtypes
import concourse.bass as bass
import concourse.mybir as mybir
from concourse.bass_utils import run_bass_kernel_spmd

F32 = mybir.dt.float32
BF16 = mybir.dt.bfloat16
FP8 = mybir.dt.float8e4
AF = mybir.ActivationFunctionType
ALU = mybir.AluOpType
AX = mybir.AxisListType

S = 2048
D = 2048
NH = 32
HD = 64
ROT = 32
NPAIR = 16
NC = 16
bfdt = ml_dtypes.bfloat16
f8dt = ml_dtypes.float8_e4m3fn


def _host_consts():
    t = np.arange(S, dtype=np.float64)
    inv = 10000.0 ** (-np.arange(0, ROT, 2, dtype=np.float64) / ROT)
    fr = np.outer(t, inv)
    cos = np.cos(fr)
    sin = np.sin(fr)
    ctab = np.zeros((128, S), dtype=np.float64)
    stab = np.zeros((128, S), dtype=np.float64)
    for r in range(128):
        hb = r % 64
        if hb < 16:
            ctab[r] = cos[:, hb]
            stab[r] = sin[:, hb]
        elif hb < 32:
            ctab[r] = cos[:, hb - 16]
            stab[r] = -sin[:, hb - 16]
        else:
            ctab[r] = 1.0
            stab[r] = 0.0
    tri = (np.arange(128)[None, :] >= np.arange(128)[:, None])
    return (ctab.astype(bfdt), stab.astype(bfdt), tri.astype(bfdt),
            np.eye(128, dtype=np.float32), np.eye(128, dtype=bfdt),
            np.eye(128, dtype=f8dt))


class Prog:
    ENGS = ("sync", "gpsimd", "tensor", "vector", "scalar")

    def __init__(self, nc, es):
        self.nc = nc
        self.ops = {e: [] for e in self.ENGS}
        self.sem = {}
        self.cnt = {}
        self.satisfied = {e: {} for e in self.ENGS}
        for e in self.ENGS:
            self.sem[e] = es.enter_context(nc.semaphore("sem_" + e))
            self.cnt[e] = 0

    def _wait(self, eng, prod, val):
        if val <= 0:
            return
        if self.satisfied[eng].get(prod, 0) >= val:
            return
        self.satisfied[eng][prod] = val
        sem = self.sem[prod]
        self.ops[eng].append(lambda e, s=sem, v=val: e.wait_ge(s, v))

    def op(self, eng, fn, reads=(), writes=(), inc=1, small=False):
        deps = {}
        need_drain = False
        for r in reads:
            for p, v in r.w.items():
                deps[p] = max(deps.get(p, 0), v)
            if r.w_eng == eng and (r.w_small or small):
                need_drain = True
        for r in writes:
            for p, v in r.w.items():
                deps[p] = max(deps.get(p, 0), v)
            for p, v in r.r.items():
                deps[p] = max(deps.get(p, 0), v)
        for p, v in deps.items():
            if p != eng:
                self._wait(eng, p, v)
        if need_drain:
            self.ops[eng].append(lambda e: e.drain())
        self.cnt[eng] += inc
        n = self.cnt[eng]
        sem = self.sem[eng]
        if inc:
            self.ops[eng].append(lambda e, f=fn, s=sem, a=inc: f(e).then_inc(s, a))
        else:
            self.ops[eng].append(fn)
        for r in writes:
            r.w = {eng: n}
            r.r = {}
            r.w_eng = eng
            r.w_small = small
        for r in reads:
            r.r[eng] = n
        return n

    def barrier(self):
        snap = dict(self.cnt)
        for e in self.ENGS:
            for p, v in snap.items():
                if p != e and v > 0:
                    self._wait(e, p, v)


class Res:
    __slots__ = ("w", "r", "w_eng", "w_small")

    def __init__(self):
        self.w = {}
        self.r = {}
        self.w_eng = None
        self.w_small = False


def build_nc(qdq_mul=4.0 / 3.0, ffn_scale=1.125, debug=False):
    nc = bass.Bass("TRN2", target_bir_lowering=False)
    from contextlib import ExitStack
    es = ExitStack()

    dX = nc.dram_tensor("x", [S, D], F32, kind="ExternalInput")
    dWqkv = nc.dram_tensor("wqkv", [D, 3 * D], F32, kind="ExternalInput")
    dWout = nc.dram_tensor("wout", [D, D], F32, kind="ExternalInput")
    dW8 = nc.dram_tensor("w8", [D, D], FP8, kind="ExternalInput")
    dBar = nc.dram_tensor("barena", [128, D], F32, kind="ExternalInput")
    dCtab = nc.dram_tensor("ctab", [128, S], BF16, kind="ExternalInput")
    dStab = nc.dram_tensor("stab", [128, S], BF16, kind="ExternalInput")
    dTri = nc.dram_tensor("tri", [128, 128], BF16, kind="ExternalInput")
    dIf = nc.dram_tensor("identf", [128, 128], F32, kind="ExternalInput")
    dIb = nc.dram_tensor("identb", [128, 128], BF16, kind="ExternalInput")
    dI8 = nc.dram_tensor("ident8", [128, 128], FP8, kind="ExternalInput")
    dY = nc.dram_tensor("y", [S, D], F32, kind="ExternalOutput")

    ps = lambda n, sh, dt: es.enter_context(nc.psum_tensor(n, sh, dt))

    ARENA = 207 * 1024
    arena = es.enter_context(nc.sbuf_tensor("sb_arena", [128, ARENA], mybir.dt.uint8))
    base = nc.lookup_mloc(arena).addr
    cur = [0]

    def _sz(sh, dt):
        n = 1
        for d in sh[1:]:
            n *= d
        return (n * mybir.dt.size(dt) + 127) & ~127

    def sb(name, sh, dt, at=None):
        off = cur[0] if at is None else at
        if at is None:
            cur[0] += _sz(sh, dt)
            assert cur[0] <= ARENA, (name, cur[0])
        return nc.alloc_sbuf_tensor_at(name, sh, dt, offset=base + off)

    _x0 = cur[0]
    xnT = sb("xnT", [128, NC * S], BF16)
    w8T = sb("w8T", [128, NC * D], FP8, at=_x0)
    WoB = sb("WoB", [128, 2 * NC * 512], BF16, at=_x0 + NC * D)

    _y0 = cur[0]
    qT = sb("qT", [128, S], BF16)
    kT = sb("kT", [128, S], BF16)
    vA = sb("vA", [128, NC * HD], BF16)
    vB = sb("vB", [128, NC * HD], BF16)
    Wq = sb("Wq", [128, NC * 128], BF16)
    Wk = sb("Wk", [128, NC * 128], BF16)
    Wv = sb("Wv", [128, NC * 128], BF16)
    probs = sb("probs", [128, 4 * 512], BF16)
    ctab = sb("ctab", [128, S], BF16)
    stab = sb("stab", [128, S], BF16)
    xq8T = sb("xq8T", [128, NC * S], FP8, at=_y0)
    xq8 = sb("xq8", [128, D], FP8, at=_y0 + NC * S)

    aT = sb("aT", [128, NC * S], BF16)
    Wst = sb("Wst", [128, 2048], F32)
    tri = sb("tri", [128, 128], BF16)
    idf = sb("idf", [128, 128], F32)
    idb = sb("idb", [128, 128], BF16)
    id8 = sb("id8", [128, 128], FP8)
    barena = sb("barena", [128, D], F32)
    onesF = sb("onesF", [128, 512], F32)
    onesB = sb("onesB", [128, 16], BF16)
    xt = sb("xt", [128, 2048], F32)
    scrF = sb("scrF", [128, 2048], F32)
    sRec = sb("sRec", [128, 512], F32)
    miscF = sb("miscF", [128, 16], F32)
    negc = sb("negc", [128, 1], F32)
    lnT = sb("lnT", [128, 12], F32)
    epsT = sb("epsT", [128, 1], F32)
    assert cur[0] <= ARENA, cur[0]
    if debug:
        print("SBUF arena used:", cur[0], "of", ARENA)

    pb = [ps(f"pb{i}", [128, 512], F32) for i in range(7)]
    p8 = ps("p8", [128, 2048], FP8)

    P = Prog(nc, es)

    R = lambda: Res()
    rXt, rWst, rWq, rWk, rWv = R(), R(), R(), R(), R()
    rQT, rKT, rVA, rVB = R(), R(), R(), R()
    rXnT, rAT = R(), R()
    rPb = [R() for _ in range(7)]
    rP8 = R()
    rProbs = [R() for _ in range(4)]
    rScrF = R()
    rSRec, rMisc, rNegc = R(), R(), R()
    rLn = R()
    rW8T, rXq8, rXq8T, rWoB = R(), R(), R(), [R(), R()]
    rConst = R()
    rY = R()

    def dma(eng, dst, src):
        return lambda e, d=dst, s=src: e.dma_start(d, s)

    for dst, src in ((ctab[:, :], dCtab.ap()), (stab[:, :], dStab.ap()),
                     (tri[:, :], dTri.ap()), (idf[:, :], dIf.ap()),
                     (idb[:, :], dIb.ap()), (id8[:, :], dI8.ap()),
                     (barena[:, :], dBar.ap())):
        P.op("sync", dma("sync", dst, src), writes=(rConst,), inc=16)
    P.op("vector", lambda e: e.memset(onesF[:, :], 1.0), writes=(rConst,))
    P.op("vector", lambda e: e.memset(onesB[:, :], 1.0), writes=(rConst,))
    P.op("vector", lambda e: e.memset(epsT[:, :], 1e-5), writes=(rConst,))

    def emit_ln():
        P.op("vector", lambda e: e.reduce_sum(lnT[:, 0:1], xt[:, :], AX.X),
             reads=(rXt,), writes=(rLn,), small=True)
        P.op("scalar", lambda e: e.mul(lnT[:, 1:2], lnT[:, 0:1], 1.0 / D),
             reads=(rLn,), writes=(rLn,), small=True)
        P.op("scalar", lambda e: e.square(scrF[:, 0:2048], xt[:, :]),
             reads=(rXt,), writes=(rScrF,))
        P.op("vector", lambda e: e.reduce_sum(lnT[:, 2:3], scrF[:, 0:2048], AX.X),
             reads=(rScrF,), writes=(rLn,), small=True)
        P.op("scalar", lambda e: e.mul(lnT[:, 3:4], lnT[:, 2:3], 1.0 / D),
             reads=(rLn,), writes=(rLn,), small=True)
        P.op("vector", lambda e: e.tensor_tensor(lnT[:, 4:5], lnT[:, 1:2],
                                                 lnT[:, 1:2], ALU.mult),
             reads=(rLn,), writes=(rLn,), small=True)
        P.op("vector", lambda e: e.tensor_tensor(lnT[:, 5:6], lnT[:, 3:4],
                                                 lnT[:, 4:5], ALU.subtract),
             reads=(rLn,), writes=(rLn,), small=True)
        P.op("vector", lambda e: e.tensor_scalar(lnT[:, 6:7], lnT[:, 5:6],
                                                 epsT[:, 0:1], None, ALU.add),
             reads=(rConst, rLn), writes=(rLn,), small=True)
        P.op("vector", lambda e: e.reciprocal(lnT[:, 7:8], lnT[:, 6:7]),
             reads=(rLn,), writes=(rLn,), small=True)
        P.op("scalar", lambda e: e.sqrt(lnT[:, 8:9], lnT[:, 7:8]),
             reads=(rLn,), writes=(rLn,), small=True)

    for tc in range(NC):
        P.op("sync", dma("sync", xt[:, :],
                         bass.AP(dX, tc * 128 * D, [[D, 128], [1, D]])),
             writes=(rXt,), inc=16)
        emit_ln()
        P.op("vector", lambda e: e.tensor_scalar(xt[:, :], xt[:, :], lnT[:, 1:2],
                                                 lnT[:, 8:9], ALU.subtract, ALU.mult),
             reads=(rLn,), writes=(rXt,))
        for i in range(NC):
            b = i // 4
            P.op("tensor", lambda e, i=i, b=b: e.transpose(
                pb[b][:, 128 * (i % 4):128 * (i % 4) + 128], xt[:, 128 * i:128 * i + 128],
                idf[:, :]),
                reads=(rXt, rConst), writes=(rPb[b],))
        for b in range(4):
            P.op("vector", lambda e, b=b, tc=tc: e.tensor_copy(
                bass.AP(xnT, 8192 * b + 128 * tc, [[NC * S, 128], [2048, 4], [1, 128]]),
                pb[b][:, :]),
                reads=(rPb[b],), writes=(rXnT,))

    dbg_out = {}

    def dbg(name, src_ap, shape, dt, res):
        if not debug:
            return
        d = nc.dram_tensor("dbg_" + name, shape, dt, kind="ExternalOutput")
        dbg_out[name] = d
        P.op("gpsimd", lambda e: e.dma_start(d.ap(), src_ap), reads=(res,), inc=16)

    dbg("xnT", xnT[:, :], [128, NC * S], BF16, rXnT)

    for p in range(NPAIR):
        for bi, (c0, rW) in enumerate(((128 * p, rWq), (D + 128 * p, rWk),
                                       (2 * D + 128 * p, rWv))):
            P.op("sync", dma("sync", bass.AP(Wst, 0, [[2048, 128], [128, 16], [1, 128]]),
                             bass.AP(dWqkv, c0, [[3 * D, 128], [128 * 3 * D, 16], [1, 128]])),
                 writes=(rWst,), inc=16)
            dstW = (Wq, Wk, Wv)[bi]
            P.op("vector", lambda e, d=dstW: e.tensor_copy(d[:, :], Wst[:, :]),
                 reads=(rWst,), writes=(rW,))

        for mi, (Wm, rWm, brow, dstT, rDst) in enumerate(
                ((Wq, rWq, 0, qT, rQT), (Wk, rWk, 32, kT, rKT))):
            for tc4 in range(4):
                bank = tc4 % 2
                P.op("tensor", lambda e, brow=brow, p=p, bank=bank: e.matmul(
                    pb[bank][:, :], barena[brow:brow + 1, 128 * p:128 * p + 128],
                    onesF[brow:brow + 1, 0:512], start=True, stop=False),
                    reads=(rConst,), writes=(rPb[bank],))
                for dc in range(NC):
                    P.op("tensor", lambda e, Wm=Wm, dc=dc, tc4=tc4, bank=bank: e.matmul(
                        pb[bank][:, :], Wm[:, 128 * dc:128 * dc + 128],
                        xnT[:, 2048 * dc + 512 * tc4:2048 * dc + 512 * tc4 + 512],
                        start=False, stop=(dc == NC - 1)),
                        reads=(rWm, rXnT), writes=(rPb[bank],))
                cs = slice(512 * tc4, 512 * tc4 + 512)
                P.op("vector", lambda e, bank=bank, cs=cs: e.tensor_tensor(
                    scrF[:, 0:512], pb[bank][:, :], ctab[:, cs], ALU.mult),
                    reads=(rPb[bank], rConst), writes=(rScrF,))
                P.op("vector", lambda e, bank=bank, cs=cs: e.tensor_tensor(
                    scrF[:, 512:1024], pb[bank][:, :], stab[:, cs], ALU.mult),
                    reads=(rPb[bank], rConst), writes=(rScrF,))
                P.op("vector", lambda e: e.stream_shuffle(
                    scrF[:, 1024:1536], scrF[:, 512:1024],
                    [(i + 16) % 32 for i in range(32)]),
                    reads=(rScrF,), writes=(rScrF,))
                P.op("vector", lambda e, dstT=dstT, cs=cs: e.tensor_tensor(
                    dstT[:, cs], scrF[:, 0:512], scrF[:, 1024:1536], ALU.add),
                    reads=(rScrF,), writes=(rDst,))

        P.op("vector", lambda e: e.memset(sRec[64:65, :], 0.0), writes=(rSRec,), small=True)
        P.op("scalar", lambda e: e.square(probs[:, 0:2048], qT[:, :]),
             reads=(rQT,), writes=(rProbs[0], rProbs[1], rProbs[2], rProbs[3]))
        for i in range(4):
            P.op("tensor", lambda e, i=i: e.matmul(
                pb[i][0:1, :], onesB[:, 0:1], probs[:, 512 * i:512 * i + 512],
                start=True, stop=True),
                reads=(rProbs[i], rConst), writes=(rPb[i],))
        for i in range(4):
            P.op("vector", lambda e, i=i: e.tensor_tensor(
                sRec[64:65, :], sRec[64:65, :], pb[i][0:1, :], ALU.max),
                reads=(rPb[i], rSRec), writes=(rSRec,), small=True)
        P.op("scalar", lambda e: e.square(probs[:, 0:2048], kT[:, :]),
             reads=(rKT,), writes=(rProbs[0], rProbs[1], rProbs[2], rProbs[3]))
        for i in range(4):
            P.op("tensor", lambda e, i=i: e.matmul(
                pb[i][0:1, :], onesB[:, 0:1], probs[:, 512 * i:512 * i + 512],
                start=True, stop=True),
                reads=(rProbs[i], rConst), writes=(rPb[i],))
        for i in range(4):
            P.op("vector", lambda e, i=i: e.tensor_tensor(
                sRec[64:65, :], sRec[64:65, :], pb[i][0:1, :], ALU.max),
                reads=(rPb[i], rSRec), writes=(rSRec,), small=True)
        P.op("vector", lambda e: e.reduce_max(miscF[64:65, 0:1], sRec[64:65, :], AX.X),
             reads=(rSRec,), writes=(rMisc,), small=True)
        P.op("scalar", lambda e: e.mul(miscF[64:65, 1:2], miscF[64:65, 0:1], -0.125),
             reads=(rMisc,), writes=(rMisc,), small=True)
        P.op("tensor", lambda e: e.matmul(pb[6][:, 0:1], onesF[64:65, 0:128],
                                          miscF[64:65, 1:2], start=True, stop=True),
             reads=(rMisc, rConst), writes=(rPb[6],))
        P.op("vector", lambda e: e.tensor_copy(negc[:, :], pb[6][:, 0:1]),
             reads=(rPb[6],), writes=(rNegc,), small=True)

        for tc4 in range(4):
            bank = tc4 % 2
            P.op("tensor", lambda e, p=p, bank=bank: e.matmul(
                pb[bank][:, :], barena[64:65, 128 * p:128 * p + 128],
                onesF[64:65, 0:512], start=True, stop=False),
                reads=(rConst,), writes=(rPb[bank],))
            for dc in range(NC):
                P.op("tensor", lambda e, dc=dc, tc4=tc4, bank=bank: e.matmul(
                    pb[bank][:, :], Wv[:, 128 * dc:128 * dc + 128],
                    xnT[:, 2048 * dc + 512 * tc4:2048 * dc + 512 * tc4 + 512],
                    start=False, stop=(dc == NC - 1)),
                    reads=(rWv, rXnT), writes=(rPb[bank],))
            P.op("vector", lambda e, bank=bank: e.tensor_copy(scrF[:, 0:512], pb[bank][:, :]),
                 reads=(rPb[bank],), writes=(rScrF,))
            for i in range(4):
                P.op("tensor", lambda e, i=i: e.transpose(
                    pb[6][:, 128 * i:128 * i + 128], scrF[:, 128 * i:128 * i + 128],
                    idf[:, :]),
                    reads=(rScrF, rConst), writes=(rPb[6],))
            for i in range(4):
                sc = tc4 * 4 + i
                P.op("vector", lambda e, i=i, sc=sc: e.tensor_copy(
                    vA[:, 64 * sc:64 * sc + 64], pb[6][:, 128 * i:128 * i + 64]),
                    reads=(rPb[6],), writes=(rVA,))
                P.op("vector", lambda e, i=i, sc=sc: e.tensor_copy(
                    vB[:, 64 * sc:64 * sc + 64], pb[6][:, 128 * i + 64:128 * i + 128]),
                    reads=(rPb[6],), writes=(rVB,))

        if p == 0:
            dbg("qT0", qT[:, :], [128, S], BF16, rQT)
            dbg("kT0", kT[:, :], [128, S], BF16, rKT)
            dbg("negc0", negc[:, :], [128, 1], F32, rNegc)
            dbg("vA0", vA[:, :], [128, NC * HD], BF16, rVA)
            dbg("vB0", vB[:, :], [128, NC * HD], BF16, rVB)

        def jlist_of(T):
            return [4 * T] + list(range(0, 4 * T)) + [4 * T + r for r in (1, 2, 3)]

        def emit_scores(T, idx):
            j = jlist_of(T)[idx]
            par = idx % 2
            bA, bB = par, 2 + par
            r0 = max(0, 128 * (j - 4 * T))
            cs = slice(r0, 512)
            P.op("tensor", lambda e, j=j, T=T, cs=cs, bA=bA: e.matmul(
                pb[bA][:, cs], kT[0:64, 128 * j:128 * j + 128],
                qT[0:64, 512 * T + cs.start:512 * T + 512],
                start=True, stop=True, tile_position=(0, 0)),
                reads=(rKT, rQT), writes=(rPb[bA],))
            P.op("tensor", lambda e, j=j, T=T, cs=cs, bB=bB: e.matmul(
                pb[bB][:, cs], kT[64:128, 128 * j:128 * j + 128],
                qT[64:128, 512 * T + cs.start:512 * T + 512],
                start=True, stop=True, tile_position=(64, 0)),
                reads=(rKT, rQT), writes=(rPb[bB],))

        def emit_expmask(T, idx):
            j = jlist_of(T)[idx]
            par = idx % 2
            bA, bB = par, 2 + par
            r0 = max(0, 128 * (j - 4 * T))
            cs = slice(r0, 512)
            pApr, pBpr = rProbs[par], rProbs[2 + par]
            P.op("scalar", lambda e, cs=cs, bA=bA, par=par: e.activation(
                probs[:, 512 * par + cs.start:512 * par + 512], pb[bA][:, cs],
                AF.Exp, bias=negc[:, 0:1], scale=0.125),
                reads=(rPb[bA], rNegc), writes=(pApr,))
            P.op("scalar", lambda e, cs=cs, bB=bB, par=par: e.activation(
                probs[:, 1024 + 512 * par + cs.start:1024 + 512 * par + 512],
                pb[bB][:, cs], AF.Exp, bias=negc[:, 0:1], scale=0.125),
                reads=(rPb[bB], rNegc), writes=(pBpr,))
            if j >= 4 * T:
                m0 = r0
                P.op("vector", lambda e, m0=m0, par=par: e.tensor_tensor(
                    probs[:, 512 * par + m0:512 * par + m0 + 128],
                    probs[:, 512 * par + m0:512 * par + m0 + 128],
                    tri[:, :], ALU.mult),
                    reads=(rConst,), writes=(pApr,))
                P.op("vector", lambda e, m0=m0, par=par: e.tensor_tensor(
                    probs[:, 1024 + 512 * par + m0:1024 + 512 * par + m0 + 128],
                    probs[:, 1024 + 512 * par + m0:1024 + 512 * par + m0 + 128],
                    tri[:, :], ALU.mult),
                    reads=(rConst,), writes=(pBpr,))

        def emit_denav(T, idx):
            j = jlist_of(T)[idx]
            par = idx % 2
            r0 = max(0, 128 * (j - 4 * T))
            cs = slice(r0, 512)
            first = (idx == 0)
            pApr, pBpr = rProbs[par], rProbs[2 + par]
            P.op("tensor", lambda e, cs=cs, par=par, first=first: e.matmul(
                pb[5][0:1, cs], onesB[:, 0:1],
                probs[:, 512 * par + cs.start:512 * par + 512],
                start=first, stop=False, tile_position=(0, 0)),
                reads=(pApr, rConst), writes=(rPb[5],))
            P.op("tensor", lambda e, cs=cs, par=par, first=first: e.matmul(
                pb[5][32:33, cs], onesB[:, 0:1],
                probs[:, 1024 + 512 * par + cs.start:1024 + 512 * par + 512],
                start=first, stop=False, tile_position=(0, 32)),
                reads=(pBpr, rConst), writes=(rPb[5],))
            P.op("tensor", lambda e, j=j, cs=cs, par=par, first=first: e.matmul(
                pb[4][0:64, cs], vA[:, 64 * j:64 * j + 64],
                probs[:, 512 * par + cs.start:512 * par + 512],
                start=first, stop=False, tile_position=(0, 0)),
                reads=(pApr, rVA), writes=(rPb[4],))
            P.op("tensor", lambda e, j=j, cs=cs, par=par, first=first: e.matmul(
                pb[4][64:128, cs], vB[:, 64 * j:64 * j + 64],
                probs[:, 1024 + 512 * par + cs.start:1024 + 512 * par + 512],
                start=first, stop=False, tile_position=(0, 64)),
                reads=(pBpr, rVB), writes=(rPb[4],))

        def emit_epilogue(T):
            P.op("vector", lambda e: e.reciprocal(sRec[0:1, :], pb[5][0:1, :]),
                 reads=(rPb[5],), writes=(rSRec,), small=True)
            P.op("vector", lambda e: e.reciprocal(sRec[32:33, :], pb[5][32:33, :]),
                 reads=(rPb[5],), writes=(rSRec,), small=True)
            P.op("tensor", lambda e: e.matmul(pb[6][0:64, :], onesF[0:1, 0:64],
                                              sRec[0:1, :], start=True, stop=True,
                                              tile_position=(0, 0)),
                 reads=(rSRec, rConst), writes=(rPb[6],))
            P.op("tensor", lambda e: e.matmul(pb[6][64:128, :], onesF[32:33, 0:64],
                                              sRec[32:33, :], start=True, stop=True,
                                              tile_position=(32, 64)),
                 reads=(rSRec, rConst), writes=(rPb[6],))
            P.op("vector", lambda e: e.tensor_copy(scrF[:, 0:512], pb[6][:, :]),
                 reads=(rPb[6],), writes=(rScrF,))
            P.op("vector", lambda e, p=p, T=T: e.tensor_tensor(
                aT[0:64, 2048 * p + 512 * T:2048 * p + 512 * T + 512],
                pb[4][0:64, :], scrF[0:64, 0:512], ALU.mult),
                reads=(rPb[4], rScrF), writes=(rAT,))
            P.op("vector", lambda e, p=p, T=T: e.tensor_tensor(
                aT[64:128, 2048 * p + 512 * T:2048 * p + 512 * T + 512],
                pb[4][64:128, :], scrF[64:128, 0:512], ALU.mult),
                reads=(rPb[4], rScrF), writes=(rAT,))

        sched = [(T, idx) for T in range(4) for idx in range(len(jlist_of(T)))]
        emit_scores(*sched[0])
        pos = 0
        for T in range(4):
            for idx in range(len(jlist_of(T))):
                emit_expmask(T, idx)
                if pos + 1 < len(sched):
                    emit_scores(*sched[pos + 1])
                emit_denav(T, idx)
                pos += 1
            emit_epilogue(T)

    dbg("aT", aT[:, :], [128, NC * S], BF16, rAT)

    P.barrier()

    for oc8 in range(NC):
        P.op("gpsimd", dma("gpsimd", xq8[:, :],
                           bass.AP(dW8, oc8 * 128 * D, [[D, 128], [1, D]])),
             writes=(rXq8,), inc=16)
        for g in range(2):
            for i in range(8):
                ic = 8 * g + i
                P.op("tensor", lambda e, ic=ic, i=i: e.transpose(
                    bass.AP(p8, 256 * i, [[2048, 128], [2, 128]]),
                    xq8[:, 128 * ic:128 * ic + 128], id8[:, :]),
                    reads=(rXq8, rConst), writes=(rP8,))
            P.op("vector", lambda e, g=g, oc8=oc8: e.tensor_copy(
                bass.AP(w8T, 2048 * 8 * g + 128 * oc8, [[NC * D, 128], [2048, 8], [1, 128]]),
                bass.AP(p8, 0, [[2048, 128], [256, 8], [2, 128]])),
                reads=(rP8,), writes=(rW8T,))

    for tc in range(NC):
        P.op("sync", dma("sync", xt[:, :],
                         bass.AP(dX, tc * 128 * D, [[D, 128], [1, D]])),
             writes=(rXt,), inc=16)
        emit_ln()
        P.op("scalar", lambda e: e.mul(lnT[:, 9:10], lnT[:, 8:9], qdq_mul),
             reads=(rLn,), writes=(rLn,), small=True)
        P.op("vector", lambda e: e.tensor_scalar(xq8[:, :], xt[:, :], lnT[:, 1:2],
                                                 lnT[:, 9:10], ALU.subtract, ALU.mult),
             reads=(rXt, rLn), writes=(rXq8,))
        for g in range(2):
            for i in range(8):
                ic = 8 * g + i
                P.op("tensor", lambda e, ic=ic, i=i: e.transpose(
                    bass.AP(p8, 256 * i, [[2048, 128], [2, 128]]),
                    xq8[:, 128 * ic:128 * ic + 128], id8[:, :]),
                    reads=(rXq8, rConst), writes=(rP8,))
            P.op("vector", lambda e, g=g, tc=tc: e.tensor_copy(
                bass.AP(xq8T, 2048 * 8 * g + 128 * tc, [[NC * S, 128], [2048, 8], [1, 128]]),
                bass.AP(p8, 0, [[2048, 128], [256, 8], [2, 128]])),
                reads=(rP8,), writes=(rXq8T,))

    for oc in range(4):
        pg = oc % 2
        for sub in range(4):
            o0 = 512 * oc + 128 * sub
            P.op("sync", dma("sync", bass.AP(Wst, 0, [[2048, 128], [128, 16], [1, 128]]),
                             bass.AP(dWout, o0, [[D, 128], [128 * D, 16], [1, 128]])),
                 writes=(rWst,), inc=16)
            P.op("vector", lambda e, pg=pg, sub=sub: e.tensor_copy(
                bass.AP(WoB, NC * 512 * pg + 128 * sub, [[2 * NC * 512, 128], [512, 16], [1, 128]]),
                Wst[:, :]),
                reads=(rWst,), writes=(rWoB[pg],))
        for tc in range(NC):
            pby = tc % 2
            pbf = 2 + (tc % 2)
            P.op("gpsimd", dma("gpsimd", scrF[:, 512 * (tc % 2):512 * (tc % 2) + 512],
                               bass.AP(dX, tc * 128 * D + 512 * oc, [[D, 128], [1, 512]])),
                 writes=(rScrF,), inc=16)
            P.op("tensor", lambda e, oc=oc, pby=pby: e.matmul(
                pb[pby][:, :], onesF[96:97, 0:128], barena[96:97, 512 * oc:512 * oc + 512],
                start=True, stop=False, tile_position=(96, 0)),
                reads=(rConst,), writes=(rPb[pby],))
            for dc in range(NC):
                P.op("tensor", lambda e, dc=dc, tc=tc, pg=pg, pby=pby: e.matmul(
                    pb[pby][:, :], aT[:, 2048 * dc + 128 * tc:2048 * dc + 128 * tc + 128],
                    WoB[:, NC * 512 * pg + 512 * dc:NC * 512 * pg + 512 * dc + 512],
                    start=False, stop=(dc == NC - 1)),
                    reads=(rAT, rWoB[pg]), writes=(rPb[pby],))
            for ic in range(NC):
                P.op("tensor", lambda e, ic=ic, tc=tc, oc=oc, pbf=pbf: e.matmul(
                    pb[pbf][:, :], xq8T[:, 2048 * ic + 128 * tc:2048 * ic + 128 * tc + 128],
                    w8T[:, 2048 * ic + 512 * oc:2048 * ic + 512 * oc + 512],
                    start=(ic == 0), stop=(ic == NC - 1)),
                    reads=(rXq8T, rW8T), writes=(rPb[pbf],))
            ts = 1024 + 512 * (tc % 2)
            P.op("vector", lambda e, tc=tc, pbf=pbf, ts=ts: e.scalar_tensor_tensor(
                scrF[:, ts:ts + 512], pb[pbf][:, :], ffn_scale,
                scrF[:, 512 * (tc % 2):512 * (tc % 2) + 512],
                ALU.mult, ALU.add),
                reads=(rPb[pbf],), writes=(rScrF,))
            P.op("vector", lambda e, pby=pby, ts=ts: e.tensor_tensor(
                scrF[:, ts:ts + 512], scrF[:, ts:ts + 512],
                pb[pby][:, :], ALU.add),
                reads=(rPb[pby],), writes=(rScrF,))
            P.op("gpsimd", lambda e, tc=tc, oc=oc, ts=ts: e.dma_start(
                bass.AP(dY, tc * 128 * D + 512 * oc, [[D, 128], [1, 512]]),
                scrF[:, ts:ts + 512]),
                reads=(rScrF,), writes=(rY,), inc=16)

    dbg("xq8T", xq8T[:, :], [128, NC * S], FP8, rXq8T)
    dbg("w8T", w8T[:, :], [128, NC * D], FP8, rW8T)

    P._wait("gpsimd", "gpsimd", P.cnt["gpsimd"])

    with nc.Block() as block:
        @block.sync
        def _(e):
            for f in P.ops["sync"]:
                f(e)

        @block.gpsimd
        def _(e):
            for f in P.ops["gpsimd"]:
                f(e)

        @block.tensor
        def _(e):
            for f in P.ops["tensor"]:
                f(e)

        @block.vector
        def _(e):
            for f in P.ops["vector"]:
                f(e)

        @block.scalar
        def _(e):
            for f in P.ops["scalar"]:
                f(e)

    return nc, es


def kernel(L0_attn_x, L0_attn_Wqkv, L0_attn_b_qkv, L0_attn_Wout, L0_attn_b_out,
           L0_ffn_weight, L0_ffn_weight_scale, L0_ffn_input_scale):
    x = np.ascontiguousarray(np.asarray(L0_attn_x), dtype=np.float32)
    wqkv = np.ascontiguousarray(np.asarray(L0_attn_Wqkv), dtype=np.float32)
    wout = np.ascontiguousarray(np.asarray(L0_attn_Wout), dtype=np.float32)
    w8 = np.asarray(L0_ffn_weight)
    if w8.dtype != f8dt:
        w8 = w8.view(f8dt) if w8.dtype.itemsize == 1 else w8.astype(f8dt)
    w8 = np.ascontiguousarray(w8)
    bq = np.asarray(L0_attn_b_qkv, dtype=np.float32)
    bo = np.asarray(L0_attn_b_out, dtype=np.float32)

    barena = np.zeros((128, D), dtype=np.float32)
    barena[0] = bq[0:D]
    barena[32] = bq[D:2 * D]
    barena[64] = bq[2 * D:3 * D]
    barena[96] = bo

    ctab, stab, tri, idf, idb, id8 = _host_consts()

    s_in = float(np.asarray(L0_ffn_input_scale, dtype=np.float32))
    s_w = float(np.asarray(L0_ffn_weight_scale, dtype=np.float32))
    nc, es = build_nc(qdq_mul=float(np.float32(1.0) / np.float32(s_in)),
                      ffn_scale=float(np.float32(s_w) * np.float32(s_in)))
    shared = {
        "wqkv": wqkv, "wout": wout, "w8": w8, "barena": barena,
        "ctab": ctab, "stab": stab, "tri": tri,
        "identf": idf, "identb": idb, "ident8": id8,
    }
    in_maps = [dict(shared, x=np.ascontiguousarray(x[c])) for c in range(8)]
    res = run_bass_kernel_spmd(nc, in_maps, core_ids=list(range(8)))
    es.close()
    out = np.stack([np.asarray(r["y"], dtype=np.float32) for r in res.results], axis=0)
    return out


# version 03
# speedup vs baseline: 1.3764x; 1.1334x over previous
import numpy as np
import ml_dtypes
import concourse.bass as bass
import concourse.mybir as mybir
from concourse.bass_utils import run_bass_kernel_spmd

F32 = mybir.dt.float32
BF16 = mybir.dt.bfloat16
FP8 = mybir.dt.float8e4
AF = mybir.ActivationFunctionType
ALU = mybir.AluOpType
AX = mybir.AxisListType

S = 2048
D = 2048
NH = 32
HD = 64
ROT = 32
NPAIR = 16
NC = 16
bfdt = ml_dtypes.bfloat16
f8dt = ml_dtypes.float8_e4m3fn


def _host_consts():
    t = np.arange(S, dtype=np.float64)
    inv = 10000.0 ** (-np.arange(0, ROT, 2, dtype=np.float64) / ROT)
    fr = np.outer(t, inv)
    cos = np.cos(fr)
    sin = np.sin(fr)
    ctab = np.zeros((128, S), dtype=np.float64)
    stab = np.zeros((128, S), dtype=np.float64)
    for r in range(128):
        hb = r % 64
        if hb < 16:
            ctab[r] = cos[:, hb]
            stab[r] = sin[:, hb]
        elif hb < 32:
            ctab[r] = cos[:, hb - 16]
            stab[r] = -sin[:, hb - 16]
        else:
            ctab[r] = 1.0
            stab[r] = 0.0
    tri = (np.arange(128)[None, :] >= np.arange(128)[:, None])
    return (ctab.astype(bfdt), stab.astype(bfdt), tri.astype(bfdt),
            np.eye(128, dtype=np.float32), np.eye(128, dtype=bfdt),
            np.eye(128, dtype=f8dt))


class Prog:
    ENGS = ("sync", "gpsimd", "tensor", "vector", "scalar")

    def __init__(self, nc, es):
        self.nc = nc
        self.ops = {e: [] for e in self.ENGS}
        self.sem = {}
        self.cnt = {}
        self.satisfied = {e: {} for e in self.ENGS}
        for e in self.ENGS:
            self.sem[e] = es.enter_context(nc.semaphore("sem_" + e))
            self.cnt[e] = 0

    def _wait(self, eng, prod, val):
        if val <= 0:
            return
        if self.satisfied[eng].get(prod, 0) >= val:
            return
        self.satisfied[eng][prod] = val
        sem = self.sem[prod]
        self.ops[eng].append(lambda e, s=sem, v=val: e.wait_ge(s, v))

    def op(self, eng, fn, reads=(), writes=(), inc=1, small=False):
        deps = {}
        need_drain = False
        for r in reads:
            for p, v in r.w.items():
                deps[p] = max(deps.get(p, 0), v)
            if r.w_eng == eng and (r.w_small or small):
                need_drain = True
        for r in writes:
            for p, v in r.w.items():
                deps[p] = max(deps.get(p, 0), v)
            for p, v in r.r.items():
                deps[p] = max(deps.get(p, 0), v)
        for p, v in deps.items():
            if p != eng:
                self._wait(eng, p, v)
        if need_drain:
            self.ops[eng].append(lambda e: e.drain())
        self.cnt[eng] += inc
        n = self.cnt[eng]
        sem = self.sem[eng]
        if inc:
            self.ops[eng].append(lambda e, f=fn, s=sem, a=inc: f(e).then_inc(s, a))
        else:
            self.ops[eng].append(fn)
        for r in writes:
            r.w = {eng: n}
            r.r = {}
            r.w_eng = eng
            r.w_small = small
        for r in reads:
            r.r[eng] = n
        return n

    def barrier(self):
        snap = dict(self.cnt)
        for e in self.ENGS:
            for p, v in snap.items():
                if p != e and v > 0:
                    self._wait(e, p, v)


class Res:
    __slots__ = ("w", "r", "w_eng", "w_small")

    def __init__(self):
        self.w = {}
        self.r = {}
        self.w_eng = None
        self.w_small = False


def build_nc(qdq_mul=4.0 / 3.0, ffn_scale=1.125, debug=False):
    nc = bass.Bass("TRN2", target_bir_lowering=False)
    from contextlib import ExitStack
    es = ExitStack()

    dX = nc.dram_tensor("x", [S, D], F32, kind="ExternalInput")
    dWqkv = nc.dram_tensor("wqkv", [D, 3 * D], F32, kind="ExternalInput")
    dWout = nc.dram_tensor("wout", [D, D], F32, kind="ExternalInput")
    dW8 = nc.dram_tensor("w8", [D, D], FP8, kind="ExternalInput")
    dBar = nc.dram_tensor("barena", [128, D], F32, kind="ExternalInput")
    dCtab = nc.dram_tensor("ctab", [128, S], BF16, kind="ExternalInput")
    dStab = nc.dram_tensor("stab", [128, S], BF16, kind="ExternalInput")
    dTri = nc.dram_tensor("tri", [128, 128], BF16, kind="ExternalInput")
    dIf = nc.dram_tensor("identf", [128, 128], F32, kind="ExternalInput")
    dIb = nc.dram_tensor("identb", [128, 128], BF16, kind="ExternalInput")
    dI8 = nc.dram_tensor("ident8", [128, 128], FP8, kind="ExternalInput")
    dY = nc.dram_tensor("y", [S, D], F32, kind="ExternalOutput")

    ps = lambda n, sh, dt: es.enter_context(nc.psum_tensor(n, sh, dt))

    ARENA = 207 * 1024
    arena = es.enter_context(nc.sbuf_tensor("sb_arena", [128, ARENA], mybir.dt.uint8))
    base = nc.lookup_mloc(arena).addr
    cur = [0]

    def _sz(sh, dt):
        n = 1
        for d in sh[1:]:
            n *= d
        return (n * mybir.dt.size(dt) + 127) & ~127

    def sb(name, sh, dt, at=None):
        off = cur[0] if at is None else at
        if at is None:
            cur[0] += _sz(sh, dt)
            assert cur[0] <= ARENA, (name, cur[0])
        return nc.alloc_sbuf_tensor_at(name, sh, dt, offset=base + off)

    _x0 = cur[0]
    xnT = sb("xnT", [128, NC * S], BF16)
    w8T = sb("w8T", [128, NC * D], FP8, at=_x0)
    WoB = sb("WoB", [128, 2 * NC * 512], BF16, at=_x0 + NC * D)

    _y0 = cur[0]
    qT = sb("qT", [128, S], BF16)
    kT = sb("kT", [128, S], BF16)
    vA = sb("vA", [128, NC * HD], BF16)
    vB = sb("vB", [128, NC * HD], BF16)
    Wq = sb("Wq", [128, NC * 128], BF16)
    Wk = sb("Wk", [128, NC * 128], BF16)
    Wv = sb("Wv", [128, NC * 128], BF16)
    probs = sb("probs", [128, 4 * 512], BF16)
    ctab = sb("ctab", [128, S], BF16)
    stab = sb("stab", [128, S], BF16)
    xq8T = sb("xq8T", [128, NC * S], FP8, at=_y0)
    xq8 = sb("xq8", [128, D], FP8, at=_y0 + NC * S)

    aT = sb("aT", [128, NC * S], BF16)
    Wst = sb("Wst", [128, 2048], F32)
    tri = sb("tri", [128, 128], BF16)
    idf = sb("idf", [128, 128], F32)
    idb = sb("idb", [128, 128], BF16)
    id8 = sb("id8", [128, 128], FP8)
    barena = sb("barena", [128, D], F32)
    onesF = sb("onesF", [128, 512], F32)
    onesB = sb("onesB", [128, 16], BF16)
    xt = sb("xt", [128, 2048], F32)
    scrF = sb("scrF", [128, 2048], F32)
    sRec = sb("sRec", [128, 512], F32)
    miscF = sb("miscF", [128, 16], F32)
    negc = sb("negc", [128, 1], F32)
    lnT = sb("lnT", [128, 12], F32)
    epsT = sb("epsT", [128, 1], F32)
    assert cur[0] <= ARENA, cur[0]
    if debug:
        print("SBUF arena used:", cur[0], "of", ARENA)

    pb = [ps(f"pb{i}", [128, 512], F32) for i in range(7)]
    p8 = ps("p8", [128, 2048], FP8)

    P = Prog(nc, es)

    R = lambda: Res()
    rXt, rWst, rWq, rWk, rWv = R(), R(), R(), R(), R()
    rQT, rKT, rVA, rVB = R(), R(), R(), R()
    rXnT, rAT = R(), R()
    rPb = [R() for _ in range(7)]
    rP8 = R()
    rProbs = [R() for _ in range(4)]
    rScrF = R()
    rSRec, rMisc, rNegc = R(), R(), R()
    rLn = R()
    rW8T, rXq8, rXq8T, rWoB = R(), R(), R(), [R(), R()]
    rConst = R()
    rY = R()

    def dma(eng, dst, src):
        return lambda e, d=dst, s=src: e.dma_start(d, s)

    for dst, src in ((ctab[:, :], dCtab.ap()), (stab[:, :], dStab.ap()),
                     (tri[:, :], dTri.ap()), (idf[:, :], dIf.ap()),
                     (idb[:, :], dIb.ap()), (id8[:, :], dI8.ap()),
                     (barena[:, :], dBar.ap())):
        P.op("sync", dma("sync", dst, src), writes=(rConst,), inc=16)
    P.op("vector", lambda e: e.memset(onesF[:, :], 1.0), writes=(rConst,))
    P.op("vector", lambda e: e.memset(onesB[:, :], 1.0), writes=(rConst,))
    P.op("vector", lambda e: e.memset(epsT[:, :], 1e-5), writes=(rConst,))

    def emit_ln():
        P.op("vector", lambda e: e.reduce_sum(lnT[:, 0:1], xt[:, :], AX.X),
             reads=(rXt,), writes=(rLn,), small=True)
        P.op("scalar", lambda e: e.mul(lnT[:, 1:2], lnT[:, 0:1], 1.0 / D),
             reads=(rLn,), writes=(rLn,), small=True)
        P.op("scalar", lambda e: e.square(scrF[:, 0:2048], xt[:, :]),
             reads=(rXt,), writes=(rScrF,))
        P.op("vector", lambda e: e.reduce_sum(lnT[:, 2:3], scrF[:, 0:2048], AX.X),
             reads=(rScrF,), writes=(rLn,), small=True)
        P.op("scalar", lambda e: e.mul(lnT[:, 3:4], lnT[:, 2:3], 1.0 / D),
             reads=(rLn,), writes=(rLn,), small=True)
        P.op("vector", lambda e: e.tensor_tensor(lnT[:, 4:5], lnT[:, 1:2],
                                                 lnT[:, 1:2], ALU.mult),
             reads=(rLn,), writes=(rLn,), small=True)
        P.op("vector", lambda e: e.tensor_tensor(lnT[:, 5:6], lnT[:, 3:4],
                                                 lnT[:, 4:5], ALU.subtract),
             reads=(rLn,), writes=(rLn,), small=True)
        P.op("vector", lambda e: e.tensor_scalar(lnT[:, 6:7], lnT[:, 5:6],
                                                 epsT[:, 0:1], None, ALU.add),
             reads=(rConst, rLn), writes=(rLn,), small=True)
        P.op("vector", lambda e: e.reciprocal(lnT[:, 7:8], lnT[:, 6:7]),
             reads=(rLn,), writes=(rLn,), small=True)
        P.op("scalar", lambda e: e.sqrt(lnT[:, 8:9], lnT[:, 7:8]),
             reads=(rLn,), writes=(rLn,), small=True)

    for tc in range(NC):
        P.op("sync", dma("sync", xt[:, :],
                         bass.AP(dX, tc * 128 * D, [[D, 128], [1, D]])),
             writes=(rXt,), inc=16)
        emit_ln()
        P.op("vector", lambda e: e.tensor_scalar(xt[:, :], xt[:, :], lnT[:, 1:2],
                                                 lnT[:, 8:9], ALU.subtract, ALU.mult),
             reads=(rLn,), writes=(rXt,))
        for i in range(NC):
            b = i // 4
            P.op("tensor", lambda e, i=i, b=b: e.transpose(
                pb[b][:, 128 * (i % 4):128 * (i % 4) + 128], xt[:, 128 * i:128 * i + 128],
                idf[:, :]),
                reads=(rXt, rConst), writes=(rPb[b],))
        for b in range(4):
            P.op("vector", lambda e, b=b, tc=tc: e.tensor_copy(
                bass.AP(xnT, 8192 * b + 128 * tc, [[NC * S, 128], [2048, 4], [1, 128]]),
                pb[b][:, :]),
                reads=(rPb[b],), writes=(rXnT,))

    dbg_out = {}

    def dbg(name, src_ap, shape, dt, res):
        if not debug:
            return
        d = nc.dram_tensor("dbg_" + name, shape, dt, kind="ExternalOutput")
        dbg_out[name] = d
        P.op("gpsimd", lambda e: e.dma_start(d.ap(), src_ap), reads=(res,), inc=16)

    dbg("xnT", xnT[:, :], [128, NC * S], BF16, rXnT)

    for p in range(NPAIR):
        for bi, (c0, rW) in enumerate(((128 * p, rWq), (D + 128 * p, rWk),
                                       (2 * D + 128 * p, rWv))):
            P.op("sync", dma("sync", bass.AP(Wst, 0, [[2048, 128], [128, 16], [1, 128]]),
                             bass.AP(dWqkv, c0, [[3 * D, 128], [128 * 3 * D, 16], [1, 128]])),
                 writes=(rWst,), inc=16)
            dstW = (Wq, Wk, Wv)[bi]
            P.op("vector", lambda e, d=dstW: e.tensor_copy(d[:, :], Wst[:, :]),
                 reads=(rWst,), writes=(rW,))

        for mi, (Wm, rWm, brow, dstT, rDst) in enumerate(
                ((Wq, rWq, 0, qT, rQT), (Wk, rWk, 32, kT, rKT))):
            for tc4 in range(4):
                bank = (mi * 4 + tc4) % 4 if False else tc4
                P.op("tensor", lambda e, brow=brow, p=p, bank=bank: e.matmul(
                    pb[bank][:, :], barena[brow:brow + 1, 128 * p:128 * p + 128],
                    onesF[brow:brow + 1, 0:512], start=True, stop=False),
                    reads=(rConst,), writes=(rPb[bank],))
                for dc in range(NC):
                    P.op("tensor", lambda e, Wm=Wm, dc=dc, tc4=tc4, bank=bank: e.matmul(
                        pb[bank][:, :], Wm[:, 128 * dc:128 * dc + 128],
                        xnT[:, 2048 * dc + 512 * tc4:2048 * dc + 512 * tc4 + 512],
                        start=False, stop=(dc == NC - 1)),
                        reads=(rWm, rXnT), writes=(rPb[bank],))
                cs = slice(512 * tc4, 512 * tc4 + 512)
                P.op("vector", lambda e, bank=bank, cs=cs: e.tensor_tensor(
                    scrF[:, 0:512], pb[bank][:, :], ctab[:, cs], ALU.mult),
                    reads=(rPb[bank], rConst), writes=(rScrF,))
                P.op("vector", lambda e, bank=bank, cs=cs: e.tensor_tensor(
                    scrF[:, 512:1024], pb[bank][:, :], stab[:, cs], ALU.mult),
                    reads=(rPb[bank], rConst), writes=(rScrF,))
                P.op("vector", lambda e: e.stream_shuffle(
                    scrF[:, 1024:1536], scrF[:, 512:1024],
                    [(i + 16) % 32 for i in range(32)]),
                    reads=(rScrF,), writes=(rScrF,))
                P.op("vector", lambda e, dstT=dstT, cs=cs: e.tensor_tensor(
                    dstT[:, cs], scrF[:, 0:512], scrF[:, 1024:1536], ALU.add),
                    reads=(rScrF,), writes=(rDst,))

        P.op("vector", lambda e: e.memset(sRec[64:65, :], 0.0), writes=(rSRec,), small=True)
        P.op("scalar", lambda e: e.square(probs[:, 0:2048], qT[:, :]),
             reads=(rQT,), writes=(rProbs[0], rProbs[1], rProbs[2], rProbs[3]))
        for i in range(4):
            P.op("tensor", lambda e, i=i: e.matmul(
                pb[i][0:1, :], onesB[:, 0:1], probs[:, 512 * i:512 * i + 512],
                start=True, stop=True),
                reads=(rProbs[i], rConst), writes=(rPb[i],))
        for i in range(4):
            P.op("vector", lambda e, i=i: e.tensor_tensor(
                sRec[64:65, :], sRec[64:65, :], pb[i][0:1, :], ALU.max),
                reads=(rPb[i], rSRec), writes=(rSRec,), small=True)
        P.op("scalar", lambda e: e.square(probs[:, 0:2048], kT[:, :]),
             reads=(rKT,), writes=(rProbs[0], rProbs[1], rProbs[2], rProbs[3]))
        for i in range(4):
            P.op("tensor", lambda e, i=i: e.matmul(
                pb[i][0:1, :], onesB[:, 0:1], probs[:, 512 * i:512 * i + 512],
                start=True, stop=True),
                reads=(rProbs[i], rConst), writes=(rPb[i],))
        for i in range(4):
            P.op("vector", lambda e, i=i: e.tensor_tensor(
                sRec[64:65, :], sRec[64:65, :], pb[i][0:1, :], ALU.max),
                reads=(rPb[i], rSRec), writes=(rSRec,), small=True)
        P.op("vector", lambda e: e.reduce_max(miscF[64:65, 0:1], sRec[64:65, :], AX.X),
             reads=(rSRec,), writes=(rMisc,), small=True)
        P.op("scalar", lambda e: e.mul(miscF[64:65, 1:2], miscF[64:65, 0:1], -0.125),
             reads=(rMisc,), writes=(rMisc,), small=True)
        P.op("tensor", lambda e: e.matmul(pb[6][:, 0:1], onesF[64:65, 0:128],
                                          miscF[64:65, 1:2], start=True, stop=True),
             reads=(rMisc, rConst), writes=(rPb[6],))
        P.op("vector", lambda e: e.tensor_copy(negc[:, :], pb[6][:, 0:1]),
             reads=(rPb[6],), writes=(rNegc,), small=True)

        pend_tr = []

        def flush_vtr():
            for fn in pend_tr:
                fn()
            del pend_tr[:]

        for tc4 in range(4):
            bank = tc4
            P.op("tensor", lambda e, p=p, bank=bank: e.matmul(
                pb[bank][:, :], barena[64:65, 128 * p:128 * p + 128],
                onesF[64:65, 0:512], start=True, stop=False),
                reads=(rConst,), writes=(rPb[bank],))
            for dc in range(NC):
                P.op("tensor", lambda e, dc=dc, tc4=tc4, bank=bank: e.matmul(
                    pb[bank][:, :], Wv[:, 128 * dc:128 * dc + 128],
                    xnT[:, 2048 * dc + 512 * tc4:2048 * dc + 512 * tc4 + 512],
                    start=False, stop=(dc == NC - 1)),
                    reads=(rWv, rXnT), writes=(rPb[bank],))
            sl = 512 * (tc4 % 2)
            P.op("vector", lambda e, bank=bank, sl=sl: e.tensor_copy(
                scrF[:, sl:sl + 512], pb[bank][:, :]),
                reads=(rPb[bank],), writes=(rScrF,))
            tb = 4 + (tc4 % 2)

            def emit_tr(tc4=tc4, sl=sl, tb=tb):
                for i in range(4):
                    P.op("tensor", lambda e, i=i, sl=sl, tb=tb: e.transpose(
                        pb[tb][:, 128 * i:128 * i + 128],
                        scrF[:, sl + 128 * i:sl + 128 * i + 128], idf[:, :]),
                        reads=(rScrF, rConst), writes=(rPb[tb],))
                for i in range(4):
                    sc = tc4 * 4 + i
                    P.op("vector", lambda e, i=i, sc=sc, tb=tb: e.tensor_copy(
                        vA[:, 64 * sc:64 * sc + 64], pb[tb][:, 128 * i:128 * i + 64]),
                        reads=(rPb[tb],), writes=(rVA,))
                    P.op("vector", lambda e, i=i, sc=sc, tb=tb: e.tensor_copy(
                        vB[:, 64 * sc:64 * sc + 64], pb[tb][:, 128 * i + 64:128 * i + 128]),
                        reads=(rPb[tb],), writes=(rVB,))
            flush_vtr()
            pend_tr.append(emit_tr)
        flush_vtr()

        if p == 0:
            dbg("qT0", qT[:, :], [128, S], BF16, rQT)
            dbg("kT0", kT[:, :], [128, S], BF16, rKT)
            dbg("negc0", negc[:, :], [128, 1], F32, rNegc)
            dbg("vA0", vA[:, :], [128, NC * HD], BF16, rVA)
            dbg("vB0", vB[:, :], [128, NC * HD], BF16, rVB)

        def jlist_of(T):
            return [4 * T] + list(range(0, 4 * T)) + [4 * T + r for r in (1, 2, 3)]

        def emit_scores(T, idx):
            j = jlist_of(T)[idx]
            par = idx % 2
            bA, bB = par, 2 + par
            r0 = max(0, 128 * (j - 4 * T))
            cs = slice(r0, 512)
            P.op("tensor", lambda e, j=j, T=T, cs=cs, bA=bA: e.matmul(
                pb[bA][:, cs], kT[0:64, 128 * j:128 * j + 128],
                qT[0:64, 512 * T + cs.start:512 * T + 512],
                start=True, stop=True, tile_position=(0, 0)),
                reads=(rKT, rQT), writes=(rPb[bA],))
            P.op("tensor", lambda e, j=j, T=T, cs=cs, bB=bB: e.matmul(
                pb[bB][:, cs], kT[64:128, 128 * j:128 * j + 128],
                qT[64:128, 512 * T + cs.start:512 * T + 512],
                start=True, stop=True, tile_position=(64, 0)),
                reads=(rKT, rQT), writes=(rPb[bB],))

        def emit_expmask(T, idx):
            j = jlist_of(T)[idx]
            par = idx % 2
            bA, bB = par, 2 + par
            r0 = max(0, 128 * (j - 4 * T))
            cs = slice(r0, 512)
            pApr, pBpr = rProbs[par], rProbs[2 + par]
            P.op("scalar", lambda e, cs=cs, bA=bA, par=par: e.activation(
                probs[:, 512 * par + cs.start:512 * par + 512], pb[bA][:, cs],
                AF.Exp, bias=negc[:, 0:1], scale=0.125),
                reads=(rPb[bA], rNegc), writes=(pApr,))
            P.op("scalar", lambda e, cs=cs, bB=bB, par=par: e.activation(
                probs[:, 1024 + 512 * par + cs.start:1024 + 512 * par + 512],
                pb[bB][:, cs], AF.Exp, bias=negc[:, 0:1], scale=0.125),
                reads=(rPb[bB], rNegc), writes=(pBpr,))
            if j >= 4 * T:
                m0 = r0
                P.op("vector", lambda e, m0=m0, par=par: e.tensor_tensor(
                    probs[:, 512 * par + m0:512 * par + m0 + 128],
                    probs[:, 512 * par + m0:512 * par + m0 + 128],
                    tri[:, :], ALU.mult),
                    reads=(rConst,), writes=(pApr,))
                P.op("vector", lambda e, m0=m0, par=par: e.tensor_tensor(
                    probs[:, 1024 + 512 * par + m0:1024 + 512 * par + m0 + 128],
                    probs[:, 1024 + 512 * par + m0:1024 + 512 * par + m0 + 128],
                    tri[:, :], ALU.mult),
                    reads=(rConst,), writes=(pBpr,))

        def emit_denav(T, idx):
            j = jlist_of(T)[idx]
            par = idx % 2
            r0 = max(0, 128 * (j - 4 * T))
            cs = slice(r0, 512)
            first = (idx == 0)
            dr = 64 * (T % 2)
            pApr, pBpr = rProbs[par], rProbs[2 + par]
            P.op("tensor", lambda e, cs=cs, par=par, first=first, dr=dr: e.matmul(
                pb[5][dr:dr + 1, cs], onesB[:, 0:1],
                probs[:, 512 * par + cs.start:512 * par + 512],
                start=first, stop=False, tile_position=(0, dr)),
                reads=(pApr, rConst), writes=(rPb[5],))
            P.op("tensor", lambda e, cs=cs, par=par, first=first, dr=dr: e.matmul(
                pb[5][dr + 32:dr + 33, cs], onesB[:, 0:1],
                probs[:, 1024 + 512 * par + cs.start:1024 + 512 * par + 512],
                start=first, stop=False, tile_position=(0, dr + 32)),
                reads=(pBpr, rConst), writes=(rPb[5],))
            P.op("tensor", lambda e, j=j, cs=cs, par=par, first=first: e.matmul(
                pb[4][0:64, cs], vA[:, 64 * j:64 * j + 64],
                probs[:, 512 * par + cs.start:512 * par + 512],
                start=first, stop=False, tile_position=(0, 0)),
                reads=(pApr, rVA), writes=(rPb[4],))
            P.op("tensor", lambda e, j=j, cs=cs, par=par, first=first: e.matmul(
                pb[4][64:128, cs], vB[:, 64 * j:64 * j + 64],
                probs[:, 1024 + 512 * par + cs.start:1024 + 512 * par + 512],
                start=first, stop=False, tile_position=(0, 64)),
                reads=(pBpr, rVB), writes=(rPb[4],))

        def emit_epi_pre(T):
            sl = 512 * (T % 2)
            dr = 64 * (T % 2)
            P.op("vector", lambda e, sl=sl: e.tensor_copy(scrF[:, sl:sl + 512],
                                                          pb[4][:, :]),
                 reads=(rPb[4],), writes=(rScrF,))
            P.op("vector", lambda e, dr=dr: e.reciprocal(sRec[0:33, :],
                                                         pb[5][dr:dr + 33, :]),
                 reads=(rPb[5],), writes=(rSRec,), small=True)

        def emit_epi_bcast(T, p):
            P.op("tensor", lambda e: e.matmul(pb[6][0:64, :], onesF[0:1, 0:64],
                                              sRec[0:1, :], start=True, stop=True,
                                              tile_position=(0, 0)),
                 reads=(rSRec, rConst), writes=(rPb[6],))
            P.op("tensor", lambda e: e.matmul(pb[6][64:128, :], onesF[32:33, 0:64],
                                              sRec[32:33, :], start=True, stop=True,
                                              tile_position=(32, 64)),
                 reads=(rSRec, rConst), writes=(rPb[6],))
            sl = 512 * (T % 2)
            P.op("vector", lambda e, p=p, T=T, sl=sl: e.tensor_tensor(
                aT[:, 2048 * p + 512 * T:2048 * p + 512 * T + 512],
                scrF[:, sl:sl + 512], pb[6][:, :], ALU.mult),
                reads=(rScrF, rPb[6]), writes=(rAT,))

        sched = [(T, idx) for T in range(4) for idx in range(len(jlist_of(T)))]
        emit_scores(*sched[0])
        pos = 0
        pend = []
        for T in range(4):
            for idx in range(len(jlist_of(T))):
                emit_expmask(T, idx)
                if pos + 1 < len(sched):
                    emit_scores(*sched[pos + 1])
                if pend and idx == 2:
                    pend.pop(0)()
                emit_denav(T, idx)
                pos += 1
            emit_epi_pre(T)
            while pend:
                pend.pop(0)()
            pend.append(lambda T=T, p=p: emit_epi_bcast(T, p))
        while pend:
            pend.pop(0)()

    dbg("aT", aT[:, :], [128, NC * S], BF16, rAT)

    P.barrier()

    for oc8 in range(NC):
        P.op("gpsimd", dma("gpsimd", xq8[:, :],
                           bass.AP(dW8, oc8 * 128 * D, [[D, 128], [1, D]])),
             writes=(rXq8,), inc=16)
        for g in range(2):
            for i in range(8):
                ic = 8 * g + i
                P.op("tensor", lambda e, ic=ic, i=i: e.transpose(
                    bass.AP(p8, 256 * i, [[2048, 128], [2, 128]]),
                    xq8[:, 128 * ic:128 * ic + 128], id8[:, :]),
                    reads=(rXq8, rConst), writes=(rP8,))
            P.op("vector", lambda e, g=g, oc8=oc8: e.tensor_copy(
                bass.AP(w8T, 2048 * 8 * g + 128 * oc8, [[NC * D, 128], [2048, 8], [1, 128]]),
                bass.AP(p8, 0, [[2048, 128], [256, 8], [2, 128]])),
                reads=(rP8,), writes=(rW8T,))

    for tc in range(NC):
        P.op("sync", dma("sync", xt[:, :],
                         bass.AP(dX, tc * 128 * D, [[D, 128], [1, D]])),
             writes=(rXt,), inc=16)
        emit_ln()
        P.op("scalar", lambda e: e.mul(lnT[:, 9:10], lnT[:, 8:9], qdq_mul),
             reads=(rLn,), writes=(rLn,), small=True)
        P.op("vector", lambda e: e.tensor_scalar(xq8[:, :], xt[:, :], lnT[:, 1:2],
                                                 lnT[:, 9:10], ALU.subtract, ALU.mult),
             reads=(rXt, rLn), writes=(rXq8,))
        for g in range(2):
            for i in range(8):
                ic = 8 * g + i
                P.op("tensor", lambda e, ic=ic, i=i: e.transpose(
                    bass.AP(p8, 256 * i, [[2048, 128], [2, 128]]),
                    xq8[:, 128 * ic:128 * ic + 128], id8[:, :]),
                    reads=(rXq8, rConst), writes=(rP8,))
            P.op("vector", lambda e, g=g, tc=tc: e.tensor_copy(
                bass.AP(xq8T, 2048 * 8 * g + 128 * tc, [[NC * S, 128], [2048, 8], [1, 128]]),
                bass.AP(p8, 0, [[2048, 128], [256, 8], [2, 128]])),
                reads=(rP8,), writes=(rXq8T,))

    for oc in range(4):
        pg = oc % 2
        for sub in range(4):
            o0 = 512 * oc + 128 * sub
            P.op("sync", dma("sync", bass.AP(Wst, 0, [[2048, 128], [128, 16], [1, 128]]),
                             bass.AP(dWout, o0, [[D, 128], [128 * D, 16], [1, 128]])),
                 writes=(rWst,), inc=16)
            P.op("vector", lambda e, pg=pg, sub=sub: e.tensor_copy(
                bass.AP(WoB, NC * 512 * pg + 128 * sub, [[2 * NC * 512, 128], [512, 16], [1, 128]]),
                Wst[:, :]),
                reads=(rWst,), writes=(rWoB[pg],))
        for tc in range(NC):
            pby = tc % 2
            pbf = 2 + (tc % 2)
            P.op("gpsimd", dma("gpsimd", scrF[:, 512 * (tc % 2):512 * (tc % 2) + 512],
                               bass.AP(dX, tc * 128 * D + 512 * oc, [[D, 128], [1, 512]])),
                 writes=(rScrF,), inc=16)
            P.op("tensor", lambda e, oc=oc, pby=pby: e.matmul(
                pb[pby][:, :], onesF[96:97, 0:128], barena[96:97, 512 * oc:512 * oc + 512],
                start=True, stop=False, tile_position=(96, 0)),
                reads=(rConst,), writes=(rPb[pby],))
            for dc in range(NC):
                P.op("tensor", lambda e, dc=dc, tc=tc, pg=pg, pby=pby: e.matmul(
                    pb[pby][:, :], aT[:, 2048 * dc + 128 * tc:2048 * dc + 128 * tc + 128],
                    WoB[:, NC * 512 * pg + 512 * dc:NC * 512 * pg + 512 * dc + 512],
                    start=False, stop=(dc == NC - 1)),
                    reads=(rAT, rWoB[pg]), writes=(rPb[pby],))
            for ic in range(NC):
                P.op("tensor", lambda e, ic=ic, tc=tc, oc=oc, pbf=pbf: e.matmul(
                    pb[pbf][:, :], xq8T[:, 2048 * ic + 128 * tc:2048 * ic + 128 * tc + 128],
                    w8T[:, 2048 * ic + 512 * oc:2048 * ic + 512 * oc + 512],
                    start=(ic == 0), stop=(ic == NC - 1)),
                    reads=(rXq8T, rW8T), writes=(rPb[pbf],))
            ts = 1024 + 512 * (tc % 2)
            P.op("vector", lambda e, tc=tc, pbf=pbf, ts=ts: e.scalar_tensor_tensor(
                scrF[:, ts:ts + 512], pb[pbf][:, :], ffn_scale,
                scrF[:, 512 * (tc % 2):512 * (tc % 2) + 512],
                ALU.mult, ALU.add),
                reads=(rPb[pbf],), writes=(rScrF,))
            P.op("vector", lambda e, pby=pby, ts=ts: e.tensor_tensor(
                scrF[:, ts:ts + 512], scrF[:, ts:ts + 512],
                pb[pby][:, :], ALU.add),
                reads=(rPb[pby],), writes=(rScrF,))
            P.op("gpsimd", lambda e, tc=tc, oc=oc, ts=ts: e.dma_start(
                bass.AP(dY, tc * 128 * D + 512 * oc, [[D, 128], [1, 512]]),
                scrF[:, ts:ts + 512]),
                reads=(rScrF,), writes=(rY,), inc=16)

    dbg("xq8T", xq8T[:, :], [128, NC * S], FP8, rXq8T)
    dbg("w8T", w8T[:, :], [128, NC * D], FP8, rW8T)

    P._wait("gpsimd", "gpsimd", P.cnt["gpsimd"])

    with nc.Block() as block:
        @block.sync
        def _(e):
            for f in P.ops["sync"]:
                f(e)

        @block.gpsimd
        def _(e):
            for f in P.ops["gpsimd"]:
                f(e)

        @block.tensor
        def _(e):
            for f in P.ops["tensor"]:
                f(e)

        @block.vector
        def _(e):
            for f in P.ops["vector"]:
                f(e)

        @block.scalar
        def _(e):
            for f in P.ops["scalar"]:
                f(e)

    return nc, es


def kernel(L0_attn_x, L0_attn_Wqkv, L0_attn_b_qkv, L0_attn_Wout, L0_attn_b_out,
           L0_ffn_weight, L0_ffn_weight_scale, L0_ffn_input_scale):
    x = np.ascontiguousarray(np.asarray(L0_attn_x), dtype=np.float32)
    wqkv = np.ascontiguousarray(np.asarray(L0_attn_Wqkv), dtype=np.float32)
    wout = np.ascontiguousarray(np.asarray(L0_attn_Wout), dtype=np.float32)
    w8 = np.asarray(L0_ffn_weight)
    if w8.dtype != f8dt:
        w8 = w8.view(f8dt) if w8.dtype.itemsize == 1 else w8.astype(f8dt)
    w8 = np.ascontiguousarray(w8)
    bq = np.asarray(L0_attn_b_qkv, dtype=np.float32)
    bo = np.asarray(L0_attn_b_out, dtype=np.float32)

    barena = np.zeros((128, D), dtype=np.float32)
    barena[0] = bq[0:D]
    barena[32] = bq[D:2 * D]
    barena[64] = bq[2 * D:3 * D]
    barena[96] = bo

    ctab, stab, tri, idf, idb, id8 = _host_consts()

    s_in = float(np.asarray(L0_ffn_input_scale, dtype=np.float32))
    s_w = float(np.asarray(L0_ffn_weight_scale, dtype=np.float32))
    nc, es = build_nc(qdq_mul=float(np.float32(1.0) / np.float32(s_in)),
                      ffn_scale=float(np.float32(s_w) * np.float32(s_in)))
    shared = {
        "wqkv": wqkv, "wout": wout, "w8": w8, "barena": barena,
        "ctab": ctab, "stab": stab, "tri": tri,
        "identf": idf, "identb": idb, "ident8": id8,
    }
    in_maps = [dict(shared, x=np.ascontiguousarray(x[c])) for c in range(8)]
    res = run_bass_kernel_spmd(nc, in_maps, core_ids=list(range(8)))
    es.close()
    out = np.stack([np.asarray(r["y"], dtype=np.float32) for r in res.results], axis=0)
    return out


# version 04
# speedup vs baseline: 1.3965x; 1.0146x over previous
import numpy as np
import ml_dtypes
import concourse.bass as bass
import concourse.mybir as mybir
from concourse.bass_utils import run_bass_kernel_spmd

F32 = mybir.dt.float32
BF16 = mybir.dt.bfloat16
FP8 = mybir.dt.float8e4
AF = mybir.ActivationFunctionType
ALU = mybir.AluOpType
AX = mybir.AxisListType

S = 2048
D = 2048
NH = 32
HD = 64
ROT = 32
NPAIR = 16
NC = 16
bfdt = ml_dtypes.bfloat16
f8dt = ml_dtypes.float8_e4m3fn


def _host_consts():
    t = np.arange(S, dtype=np.float64)
    inv = 10000.0 ** (-np.arange(0, ROT, 2, dtype=np.float64) / ROT)
    fr = np.outer(t, inv)
    cos = np.cos(fr)
    sin = np.sin(fr)
    ctab = np.zeros((128, S), dtype=np.float64)
    stab = np.zeros((128, S), dtype=np.float64)
    for r in range(128):
        hb = r % 64
        if hb < 16:
            ctab[r] = cos[:, hb]
            stab[r] = sin[:, hb]
        elif hb < 32:
            ctab[r] = cos[:, hb - 16]
            stab[r] = -sin[:, hb - 16]
        else:
            ctab[r] = 1.0
            stab[r] = 0.0
    tri = (np.arange(128)[None, :] >= np.arange(128)[:, None])
    return (ctab.astype(bfdt), stab.astype(bfdt), tri.astype(bfdt),
            np.eye(128, dtype=np.float32), np.eye(128, dtype=bfdt),
            np.eye(128, dtype=f8dt))


class Prog:
    ENGS = ("sync", "gpsimd", "tensor", "vector", "scalar")

    def __init__(self, nc, es):
        self.nc = nc
        self.ops = {e: [] for e in self.ENGS}
        self.sem = {}
        self.cnt = {}
        self.satisfied = {e: {} for e in self.ENGS}
        for e in self.ENGS:
            self.sem[e] = es.enter_context(nc.semaphore("sem_" + e))
            self.cnt[e] = 0

    def _wait(self, eng, prod, val):
        if val <= 0:
            return
        if self.satisfied[eng].get(prod, 0) >= val:
            return
        self.satisfied[eng][prod] = val
        sem = self.sem[prod]
        self.ops[eng].append(lambda e, s=sem, v=val: e.wait_ge(s, v))

    def op(self, eng, fn, reads=(), writes=(), inc=1, small=False):
        deps = {}
        need_drain = False
        for r in reads:
            for p, v in r.w.items():
                deps[p] = max(deps.get(p, 0), v)
            if r.w_eng == eng and (r.w_small or small):
                need_drain = True
        for r in writes:
            for p, v in r.w.items():
                deps[p] = max(deps.get(p, 0), v)
            for p, v in r.r.items():
                deps[p] = max(deps.get(p, 0), v)
        for p, v in deps.items():
            if p != eng:
                self._wait(eng, p, v)
        if need_drain:
            self.ops[eng].append(lambda e: e.drain())
        self.cnt[eng] += inc
        n = self.cnt[eng]
        sem = self.sem[eng]
        if inc:
            self.ops[eng].append(lambda e, f=fn, s=sem, a=inc: f(e).then_inc(s, a))
        else:
            self.ops[eng].append(fn)
        for r in writes:
            r.w = {eng: n}
            r.r = {}
            r.w_eng = eng
            r.w_small = small
        for r in reads:
            r.r[eng] = n
        return n

    def barrier(self):
        snap = dict(self.cnt)
        for e in self.ENGS:
            for p, v in snap.items():
                if p != e and v > 0:
                    self._wait(e, p, v)


class Res:
    __slots__ = ("w", "r", "w_eng", "w_small")

    def __init__(self):
        self.w = {}
        self.r = {}
        self.w_eng = None
        self.w_small = False


def build_nc(qdq_mul=4.0 / 3.0, ffn_scale=1.125, debug=False):
    nc = bass.Bass("TRN2", target_bir_lowering=False)
    from contextlib import ExitStack
    es = ExitStack()

    dX = nc.dram_tensor("x", [S, D], F32, kind="ExternalInput")
    dWqkv = nc.dram_tensor("wqkv", [D, 3 * D], F32, kind="ExternalInput")
    dWout = nc.dram_tensor("wout", [D, D], F32, kind="ExternalInput")
    dW8 = nc.dram_tensor("w8", [D, D], FP8, kind="ExternalInput")
    dBar = nc.dram_tensor("barena", [128, D], F32, kind="ExternalInput")
    dCtab = nc.dram_tensor("ctab", [128, S], BF16, kind="ExternalInput")
    dStab = nc.dram_tensor("stab", [128, S], BF16, kind="ExternalInput")
    dTri = nc.dram_tensor("tri", [128, 128], BF16, kind="ExternalInput")
    dIf = nc.dram_tensor("identf", [128, 128], F32, kind="ExternalInput")
    dIb = nc.dram_tensor("identb", [128, 128], BF16, kind="ExternalInput")
    dI8 = nc.dram_tensor("ident8", [128, 128], FP8, kind="ExternalInput")
    dY = nc.dram_tensor("y", [S, D], F32, kind="ExternalOutput")

    ps = lambda n, sh, dt: es.enter_context(nc.psum_tensor(n, sh, dt))

    ARENA = 207 * 1024
    arena = es.enter_context(nc.sbuf_tensor("sb_arena", [128, ARENA], mybir.dt.uint8))
    base = nc.lookup_mloc(arena).addr
    cur = [0]

    def _sz(sh, dt):
        n = 1
        for d in sh[1:]:
            n *= d
        return (n * mybir.dt.size(dt) + 127) & ~127

    def sb(name, sh, dt, at=None):
        off = cur[0] if at is None else at
        if at is None:
            cur[0] += _sz(sh, dt)
            assert cur[0] <= ARENA, (name, cur[0])
        return nc.alloc_sbuf_tensor_at(name, sh, dt, offset=base + off)

    _x0 = cur[0]
    xnT = sb("xnT", [128, NC * S], BF16)
    w8T = sb("w8T", [128, NC * D], FP8, at=_x0)
    WoB = sb("WoB", [128, 2 * NC * 512], BF16, at=_x0 + NC * D)

    _y0 = cur[0]
    qT = sb("qT", [128, S], BF16)
    kT = sb("kT", [128, S], BF16)
    vA = sb("vA", [128, NC * HD], BF16)
    vB = sb("vB", [128, NC * HD], BF16)
    Wq = sb("Wq", [128, NC * 128], BF16)
    Wk = sb("Wk", [128, NC * 128], BF16)
    Wv = sb("Wv", [128, NC * 128], BF16)
    probs = sb("probs", [128, 4 * 512], BF16)
    ctab = sb("ctab", [128, S], BF16)
    stab = sb("stab", [128, S], BF16)
    xq8T = sb("xq8T", [128, NC * S], FP8, at=_y0)
    xq8 = sb("xq8", [128, D], FP8, at=_y0 + NC * S)
    xt2 = sb("xt2", [128, 2048], F32, at=_y0)

    aT = sb("aT", [128, NC * S], BF16)
    Wst = sb("Wst", [128, 2048], F32)
    tri = sb("tri", [128, 128], BF16)
    idf = sb("idf", [128, 128], F32)
    idb = sb("idb", [128, 128], BF16)
    id8 = sb("id8", [128, 128], FP8)
    barena = sb("barena", [128, D], F32)
    onesF = sb("onesF", [128, 512], F32)
    onesB = sb("onesB", [128, 16], BF16)
    xt = sb("xt", [128, 2048], F32)
    scrF = sb("scrF", [128, 2048], F32)
    sRec = sb("sRec", [128, 512], F32)
    miscF = sb("miscF", [128, 16], F32)
    negc = sb("negc", [128, 1], F32)
    lnT = sb("lnT", [128, 24], F32)
    epsT = sb("epsT", [128, 1], F32)
    w8st = sb("w8st", [128, D], FP8)
    assert cur[0] <= ARENA, cur[0]
    if debug:
        print("SBUF arena used:", cur[0], "of", ARENA)

    pb = [ps(f"pb{i}", [128, 512], F32) for i in range(7)]
    p8 = ps("p8", [128, 2048], FP8)

    P = Prog(nc, es)

    R = lambda: Res()
    rXt, rWst, rWq, rWk, rWv = R(), R(), R(), R(), R()
    rQT, rKT, rVA, rVB = R(), R(), R(), R()
    rXnT, rAT = R(), R()
    rPb = [R() for _ in range(7)]
    rP8 = R()
    rProbs = [R() for _ in range(4)]
    rScrF = R()
    rSRec, rMisc, rNegc = R(), R(), R()
    rLn = R()
    rW8T, rXq8, rXq8T, rWoB = R(), R(), R(), [R(), R()]
    rXt2, rLn2, rW8st = R(), R(), R()
    rConst = R()
    rY = R()

    def dma(eng, dst, src):
        return lambda e, d=dst, s=src: e.dma_start(d, s)

    for dst, src in ((ctab[:, :], dCtab.ap()), (stab[:, :], dStab.ap()),
                     (tri[:, :], dTri.ap()), (idf[:, :], dIf.ap()),
                     (idb[:, :], dIb.ap()), (id8[:, :], dI8.ap()),
                     (barena[:, :], dBar.ap())):
        P.op("sync", dma("sync", dst, src), writes=(rConst,), inc=16)
    P.op("vector", lambda e: e.memset(onesF[:, :], 1.0), writes=(rConst,))
    P.op("vector", lambda e: e.memset(onesB[:, :], 1.0), writes=(rConst,))
    P.op("vector", lambda e: e.memset(epsT[:, :], 1e-5), writes=(rConst,))

    def emit_ln(buf=None, rB=None, c0=0, rL=None):
        buf = xt if buf is None else buf
        rB = rXt if rB is None else rB
        rL = rLn if rL is None else rL
        P.op("vector", lambda e, c0=c0, buf=buf: e.reduce_sum(lnT[:, c0:c0 + 1], buf[:, :], AX.X),
             reads=(rB,), writes=(rL,), small=True)
        P.op("scalar", lambda e, c0=c0: e.mul(lnT[:, c0 + 1:c0 + 2], lnT[:, c0:c0 + 1], 1.0 / D),
             reads=(rL,), writes=(rL,), small=True)
        P.op("scalar", lambda e, buf=buf: e.square(scrF[:, 0:2048], buf[:, :]),
             reads=(rB,), writes=(rScrF,))
        P.op("vector", lambda e, c0=c0: e.reduce_sum(lnT[:, c0 + 2:c0 + 3], scrF[:, 0:2048], AX.X),
             reads=(rScrF,), writes=(rL,), small=True)
        P.op("scalar", lambda e, c0=c0: e.mul(lnT[:, c0 + 3:c0 + 4], lnT[:, c0 + 2:c0 + 3], 1.0 / D),
             reads=(rL,), writes=(rL,), small=True)
        P.op("vector", lambda e, c0=c0: e.tensor_tensor(lnT[:, c0 + 4:c0 + 5], lnT[:, c0 + 1:c0 + 2],
                                                 lnT[:, c0 + 1:c0 + 2], ALU.mult),
             reads=(rL,), writes=(rL,), small=True)
        P.op("vector", lambda e, c0=c0: e.tensor_tensor(lnT[:, c0 + 5:c0 + 6], lnT[:, c0 + 3:c0 + 4],
                                                 lnT[:, c0 + 4:c0 + 5], ALU.subtract),
             reads=(rL,), writes=(rL,), small=True)
        P.op("vector", lambda e, c0=c0: e.tensor_scalar(lnT[:, c0 + 6:c0 + 7], lnT[:, c0 + 5:c0 + 6],
                                                 epsT[:, 0:1], None, ALU.add),
             reads=(rConst, rL), writes=(rL,), small=True)
        P.op("vector", lambda e, c0=c0: e.reciprocal(lnT[:, c0 + 7:c0 + 8], lnT[:, c0 + 6:c0 + 7]),
             reads=(rL,), writes=(rL,), small=True)
        P.op("scalar", lambda e, c0=c0: e.sqrt(lnT[:, c0 + 8:c0 + 9], lnT[:, c0 + 7:c0 + 8]),
             reads=(rL,), writes=(rL,), small=True)

    for tc in range(NC):
        buf, rB = (xt, rXt) if tc % 2 == 0 else (xt2, rXt2)
        c0 = 12 * (tc % 2)
        rL = rLn if tc % 2 == 0 else rLn2
        P.op("sync", dma("sync", buf[:, :],
                         bass.AP(dX, tc * 128 * D, [[D, 128], [1, D]])),
             writes=(rB,), inc=16)
        emit_ln(buf, rB, c0, rL)
        P.op("vector", lambda e, buf=buf, c0=c0: e.tensor_scalar(
            buf[:, :], buf[:, :], lnT[:, c0 + 1:c0 + 2],
            lnT[:, c0 + 8:c0 + 9], ALU.subtract, ALU.mult),
            reads=(rL,), writes=(rB,))
        for i in range(NC):
            b = i // 4
            P.op("tensor", lambda e, i=i, b=b, buf=buf: e.transpose(
                pb[b][:, 128 * (i % 4):128 * (i % 4) + 128], buf[:, 128 * i:128 * i + 128],
                idf[:, :]),
                reads=(rB, rConst), writes=(rPb[b],))
        for b in range(4):
            P.op("vector", lambda e, b=b, tc=tc: e.tensor_copy(
                bass.AP(xnT, 8192 * b + 128 * tc, [[NC * S, 128], [2048, 4], [1, 128]]),
                pb[b][:, :]),
                reads=(rPb[b],), writes=(rXnT,))

    dbg_out = {}

    def dbg(name, src_ap, shape, dt, res):
        if not debug:
            return
        d = nc.dram_tensor("dbg_" + name, shape, dt, kind="ExternalOutput")
        dbg_out[name] = d
        P.op("gpsimd", lambda e: e.dma_start(d.ap(), src_ap), reads=(res,), inc=16)

    dbg("xnT", xnT[:, :], [128, NC * S], BF16, rXnT)

    def emit_w8T_build():
        firstref = [True]
        for oc8 in range(NC):
            P.op("gpsimd", dma("gpsimd", w8st[:, :],
                               bass.AP(dW8, oc8 * 128 * D, [[D, 128], [1, D]])),
                 writes=(rW8st,), inc=16)
            for g in range(2):
                for i in range(8):
                    ic = 8 * g + i
                    P.op("tensor", lambda e, ic=ic, i=i: e.transpose(
                        bass.AP(p8, 256 * i, [[2048, 128], [2, 128]]),
                        w8st[:, 128 * ic:128 * ic + 128], id8[:, :]),
                        reads=(rW8st, rConst), writes=(rP8,))
                wr = (rW8T, rXnT) if firstref[0] else (rW8T,)
                firstref[0] = False
                P.op("vector", lambda e, g=g, oc8=oc8: e.tensor_copy(
                    bass.AP(w8T, 2048 * 8 * g + 128 * oc8, [[NC * D, 128], [2048, 8], [1, 128]]),
                    bass.AP(p8, 0, [[2048, 128], [256, 8], [2, 128]])),
                    reads=(rP8,), writes=wr)

    def emit_wout_load(oc):
        pg = oc % 2
        for sub in range(4):
            o0 = 512 * oc + 128 * sub
            P.op("sync", dma("sync", bass.AP(Wst, 0, [[2048, 128], [128, 16], [1, 128]]),
                             bass.AP(dWout, o0, [[D, 128], [128 * D, 16], [1, 128]])),
                 writes=(rWst,), inc=16)
            wr = (rWoB[pg], rXnT) if oc == 0 and sub == 0 else (rWoB[pg],)
            P.op("vector", lambda e, pg=pg, sub=sub: e.tensor_copy(
                bass.AP(WoB, NC * 512 * pg + 128 * sub, [[2 * NC * 512, 128], [512, 16], [1, 128]]),
                Wst[:, :]),
                reads=(rWst,), writes=wr)

    for p in range(NPAIR):
        for bi, (c0, rW) in enumerate(((128 * p, rWq), (D + 128 * p, rWk),
                                       (2 * D + 128 * p, rWv))):
            P.op("sync", dma("sync", bass.AP(Wst, 0, [[2048, 128], [128, 16], [1, 128]]),
                             bass.AP(dWqkv, c0, [[3 * D, 128], [128 * 3 * D, 16], [1, 128]])),
                 writes=(rWst,), inc=16)
            dstW = (Wq, Wk, Wv)[bi]
            P.op("vector", lambda e, d=dstW: e.tensor_copy(d[:, :], Wst[:, :]),
                 reads=(rWst,), writes=(rW,))

        for mi, (Wm, rWm, brow, dstT, rDst) in enumerate(
                ((Wq, rWq, 0, qT, rQT), (Wk, rWk, 32, kT, rKT))):
            for tc4 in range(4):
                bank = (mi * 4 + tc4) % 4 if False else tc4
                P.op("tensor", lambda e, brow=brow, p=p, bank=bank: e.matmul(
                    pb[bank][:, :], barena[brow:brow + 1, 128 * p:128 * p + 128],
                    onesF[brow:brow + 1, 0:512], start=True, stop=False),
                    reads=(rConst,), writes=(rPb[bank],))
                for dc in range(NC):
                    P.op("tensor", lambda e, Wm=Wm, dc=dc, tc4=tc4, bank=bank: e.matmul(
                        pb[bank][:, :], Wm[:, 128 * dc:128 * dc + 128],
                        xnT[:, 2048 * dc + 512 * tc4:2048 * dc + 512 * tc4 + 512],
                        start=False, stop=(dc == NC - 1)),
                        reads=(rWm, rXnT), writes=(rPb[bank],))
                cs = slice(512 * tc4, 512 * tc4 + 512)
                P.op("vector", lambda e, bank=bank, cs=cs: e.tensor_tensor(
                    scrF[:, 0:512], pb[bank][:, :], ctab[:, cs], ALU.mult),
                    reads=(rPb[bank], rConst), writes=(rScrF,))
                P.op("vector", lambda e, bank=bank, cs=cs: e.tensor_tensor(
                    scrF[:, 512:1024], pb[bank][:, :], stab[:, cs], ALU.mult),
                    reads=(rPb[bank], rConst), writes=(rScrF,))
                P.op("vector", lambda e: e.stream_shuffle(
                    scrF[:, 1024:1536], scrF[:, 512:1024],
                    [(i + 16) % 32 for i in range(32)]),
                    reads=(rScrF,), writes=(rScrF,))
                P.op("vector", lambda e, dstT=dstT, cs=cs: e.tensor_tensor(
                    dstT[:, cs], scrF[:, 0:512], scrF[:, 1024:1536], ALU.add),
                    reads=(rScrF,), writes=(rDst,))

        P.op("vector", lambda e: e.memset(sRec[64:65, :], 0.0), writes=(rSRec,), small=True)
        P.op("scalar", lambda e: e.square(probs[:, 0:2048], qT[:, :]),
             reads=(rQT,), writes=(rProbs[0], rProbs[1], rProbs[2], rProbs[3]))
        for i in range(4):
            P.op("tensor", lambda e, i=i: e.matmul(
                pb[i][0:1, :], onesB[:, 0:1], probs[:, 512 * i:512 * i + 512],
                start=True, stop=True),
                reads=(rProbs[i], rConst), writes=(rPb[i],))
        for i in range(4):
            P.op("vector", lambda e, i=i: e.tensor_tensor(
                sRec[64:65, :], sRec[64:65, :], pb[i][0:1, :], ALU.max),
                reads=(rPb[i], rSRec), writes=(rSRec,), small=True)
        P.op("scalar", lambda e: e.square(probs[:, 0:2048], kT[:, :]),
             reads=(rKT,), writes=(rProbs[0], rProbs[1], rProbs[2], rProbs[3]))
        for i in range(4):
            P.op("tensor", lambda e, i=i: e.matmul(
                pb[i][0:1, :], onesB[:, 0:1], probs[:, 512 * i:512 * i + 512],
                start=True, stop=True),
                reads=(rProbs[i], rConst), writes=(rPb[i],))
        for i in range(4):
            P.op("vector", lambda e, i=i: e.tensor_tensor(
                sRec[64:65, :], sRec[64:65, :], pb[i][0:1, :], ALU.max),
                reads=(rPb[i], rSRec), writes=(rSRec,), small=True)
        P.op("vector", lambda e: e.reduce_max(miscF[64:65, 0:1], sRec[64:65, :], AX.X),
             reads=(rSRec,), writes=(rMisc,), small=True)
        P.op("scalar", lambda e: e.mul(miscF[64:65, 1:2], miscF[64:65, 0:1], -0.125),
             reads=(rMisc,), writes=(rMisc,), small=True)
        P.op("tensor", lambda e: e.matmul(pb[6][:, 0:1], onesF[64:65, 0:128],
                                          miscF[64:65, 1:2], start=True, stop=True),
             reads=(rMisc, rConst), writes=(rPb[6],))
        P.op("vector", lambda e: e.tensor_copy(negc[:, :], pb[6][:, 0:1]),
             reads=(rPb[6],), writes=(rNegc,), small=True)

        pend_tr = []

        def flush_vtr():
            for fn in pend_tr:
                fn()
            del pend_tr[:]

        for tc4 in range(4):
            bank = tc4
            P.op("tensor", lambda e, p=p, bank=bank: e.matmul(
                pb[bank][:, :], barena[64:65, 128 * p:128 * p + 128],
                onesF[64:65, 0:512], start=True, stop=False),
                reads=(rConst,), writes=(rPb[bank],))
            for dc in range(NC):
                P.op("tensor", lambda e, dc=dc, tc4=tc4, bank=bank: e.matmul(
                    pb[bank][:, :], Wv[:, 128 * dc:128 * dc + 128],
                    xnT[:, 2048 * dc + 512 * tc4:2048 * dc + 512 * tc4 + 512],
                    start=False, stop=(dc == NC - 1)),
                    reads=(rWv, rXnT), writes=(rPb[bank],))
            sl = 512 * (tc4 % 2)
            P.op("vector", lambda e, bank=bank, sl=sl: e.tensor_copy(
                scrF[:, sl:sl + 512], pb[bank][:, :]),
                reads=(rPb[bank],), writes=(rScrF,))
            tb = 4 + (tc4 % 2)

            def emit_tr(tc4=tc4, sl=sl, tb=tb):
                for i in range(4):
                    P.op("tensor", lambda e, i=i, sl=sl, tb=tb: e.transpose(
                        pb[tb][:, 128 * i:128 * i + 128],
                        scrF[:, sl + 128 * i:sl + 128 * i + 128], idf[:, :]),
                        reads=(rScrF, rConst), writes=(rPb[tb],))
                for i in range(4):
                    sc = tc4 * 4 + i
                    P.op("vector", lambda e, i=i, sc=sc, tb=tb: e.tensor_copy(
                        vA[:, 64 * sc:64 * sc + 64], pb[tb][:, 128 * i:128 * i + 64]),
                        reads=(rPb[tb],), writes=(rVA,))
                    P.op("vector", lambda e, i=i, sc=sc, tb=tb: e.tensor_copy(
                        vB[:, 64 * sc:64 * sc + 64], pb[tb][:, 128 * i + 64:128 * i + 128]),
                        reads=(rPb[tb],), writes=(rVB,))
            flush_vtr()
            pend_tr.append(emit_tr)
        flush_vtr()

        if p == 0:
            dbg("qT0", qT[:, :], [128, S], BF16, rQT)
            dbg("kT0", kT[:, :], [128, S], BF16, rKT)
            dbg("negc0", negc[:, :], [128, 1], F32, rNegc)
            dbg("vA0", vA[:, :], [128, NC * HD], BF16, rVA)
            dbg("vB0", vB[:, :], [128, NC * HD], BF16, rVB)

        if p == NPAIR - 1:
            emit_w8T_build()
            emit_wout_load(0)

        def jlist_of(T):
            return [4 * T] + list(range(0, 4 * T)) + [4 * T + r for r in (1, 2, 3)]

        def emit_scores(T, idx):
            j = jlist_of(T)[idx]
            par = idx % 2
            bA, bB = par, 2 + par
            r0 = max(0, 128 * (j - 4 * T))
            cs = slice(r0, 512)
            P.op("tensor", lambda e, j=j, T=T, cs=cs, bA=bA: e.matmul(
                pb[bA][:, cs], kT[0:64, 128 * j:128 * j + 128],
                qT[0:64, 512 * T + cs.start:512 * T + 512],
                start=True, stop=True, tile_position=(0, 0)),
                reads=(rKT, rQT), writes=(rPb[bA],))
            P.op("tensor", lambda e, j=j, T=T, cs=cs, bB=bB: e.matmul(
                pb[bB][:, cs], kT[64:128, 128 * j:128 * j + 128],
                qT[64:128, 512 * T + cs.start:512 * T + 512],
                start=True, stop=True, tile_position=(64, 0)),
                reads=(rKT, rQT), writes=(rPb[bB],))

        def emit_expmask(T, idx):
            j = jlist_of(T)[idx]
            par = idx % 2
            bA, bB = par, 2 + par
            r0 = max(0, 128 * (j - 4 * T))
            cs = slice(r0, 512)
            pApr, pBpr = rProbs[par], rProbs[2 + par]
            P.op("scalar", lambda e, cs=cs, bA=bA, par=par: e.activation(
                probs[:, 512 * par + cs.start:512 * par + 512], pb[bA][:, cs],
                AF.Exp, bias=negc[:, 0:1], scale=0.125),
                reads=(rPb[bA], rNegc), writes=(pApr,))
            P.op("scalar", lambda e, cs=cs, bB=bB, par=par: e.activation(
                probs[:, 1024 + 512 * par + cs.start:1024 + 512 * par + 512],
                pb[bB][:, cs], AF.Exp, bias=negc[:, 0:1], scale=0.125),
                reads=(rPb[bB], rNegc), writes=(pBpr,))
            if j >= 4 * T:
                m0 = r0
                P.op("vector", lambda e, m0=m0, par=par: e.tensor_tensor(
                    probs[:, 512 * par + m0:512 * par + m0 + 128],
                    probs[:, 512 * par + m0:512 * par + m0 + 128],
                    tri[:, :], ALU.mult),
                    reads=(rConst,), writes=(pApr,))
                P.op("vector", lambda e, m0=m0, par=par: e.tensor_tensor(
                    probs[:, 1024 + 512 * par + m0:1024 + 512 * par + m0 + 128],
                    probs[:, 1024 + 512 * par + m0:1024 + 512 * par + m0 + 128],
                    tri[:, :], ALU.mult),
                    reads=(rConst,), writes=(pBpr,))

        def emit_denav(T, idx):
            j = jlist_of(T)[idx]
            par = idx % 2
            r0 = max(0, 128 * (j - 4 * T))
            cs = slice(r0, 512)
            first = (idx == 0)
            dr = 64 * (T % 2)
            pApr, pBpr = rProbs[par], rProbs[2 + par]
            P.op("tensor", lambda e, cs=cs, par=par, first=first, dr=dr: e.matmul(
                pb[5][dr:dr + 1, cs], onesB[:, 0:1],
                probs[:, 512 * par + cs.start:512 * par + 512],
                start=first, stop=False, tile_position=(0, dr)),
                reads=(pApr, rConst), writes=(rPb[5],))
            P.op("tensor", lambda e, cs=cs, par=par, first=first, dr=dr: e.matmul(
                pb[5][dr + 32:dr + 33, cs], onesB[:, 0:1],
                probs[:, 1024 + 512 * par + cs.start:1024 + 512 * par + 512],
                start=first, stop=False, tile_position=(0, dr + 32)),
                reads=(pBpr, rConst), writes=(rPb[5],))
            P.op("tensor", lambda e, j=j, cs=cs, par=par, first=first: e.matmul(
                pb[4][0:64, cs], vA[:, 64 * j:64 * j + 64],
                probs[:, 512 * par + cs.start:512 * par + 512],
                start=first, stop=False, tile_position=(0, 0)),
                reads=(pApr, rVA), writes=(rPb[4],))
            P.op("tensor", lambda e, j=j, cs=cs, par=par, first=first: e.matmul(
                pb[4][64:128, cs], vB[:, 64 * j:64 * j + 64],
                probs[:, 1024 + 512 * par + cs.start:1024 + 512 * par + 512],
                start=first, stop=False, tile_position=(0, 64)),
                reads=(pBpr, rVB), writes=(rPb[4],))

        def emit_epi_pre(T):
            sl = 512 * (T % 2)
            dr = 64 * (T % 2)
            P.op("vector", lambda e, sl=sl: e.tensor_copy(scrF[:, sl:sl + 512],
                                                          pb[4][:, :]),
                 reads=(rPb[4],), writes=(rScrF,))
            P.op("vector", lambda e, dr=dr: e.reciprocal(sRec[0:33, :],
                                                         pb[5][dr:dr + 33, :]),
                 reads=(rPb[5],), writes=(rSRec,), small=True)

        def emit_epi_bcast(T, p):
            P.op("tensor", lambda e: e.matmul(pb[6][0:64, :], onesF[0:1, 0:64],
                                              sRec[0:1, :], start=True, stop=True,
                                              tile_position=(0, 0)),
                 reads=(rSRec, rConst), writes=(rPb[6],))
            P.op("tensor", lambda e: e.matmul(pb[6][64:128, :], onesF[32:33, 0:64],
                                              sRec[32:33, :], start=True, stop=True,
                                              tile_position=(32, 64)),
                 reads=(rSRec, rConst), writes=(rPb[6],))
            sl = 512 * (T % 2)
            P.op("vector", lambda e, p=p, T=T, sl=sl: e.tensor_tensor(
                aT[:, 2048 * p + 512 * T:2048 * p + 512 * T + 512],
                scrF[:, sl:sl + 512], pb[6][:, :], ALU.mult),
                reads=(rScrF, rPb[6]), writes=(rAT,))

        sched = [(T, idx) for T in range(4) for idx in range(len(jlist_of(T)))]
        emit_scores(*sched[0])
        pos = 0
        pend = []
        for T in range(4):
            for idx in range(len(jlist_of(T))):
                emit_expmask(T, idx)
                if pos + 1 < len(sched):
                    emit_scores(*sched[pos + 1])
                if pend and idx == 2:
                    pend.pop(0)()
                emit_denav(T, idx)
                pos += 1
            emit_epi_pre(T)
            while pend:
                pend.pop(0)()
            pend.append(lambda T=T, p=p: emit_epi_bcast(T, p))
        while pend:
            pend.pop(0)()

    dbg("aT", aT[:, :], [128, NC * S], BF16, rAT)

    P.barrier()

    for oc in range(4):
        pg = oc % 2
        if oc + 1 < 4:
            emit_wout_load(oc + 1)
        for tc in range(NC):
            if oc == 0:
                P.op("sync", dma("sync", xt[:, :],
                                 bass.AP(dX, tc * 128 * D, [[D, 128], [1, D]])),
                     writes=(rXt,), inc=16)
                emit_ln()
                P.op("scalar", lambda e: e.mul(lnT[:, 9:10], lnT[:, 8:9], qdq_mul),
                     reads=(rLn,), writes=(rLn,), small=True)
                P.op("vector", lambda e: e.tensor_scalar(xq8[:, :], xt[:, :], lnT[:, 1:2],
                                                         lnT[:, 9:10], ALU.subtract, ALU.mult),
                     reads=(rXt, rLn), writes=(rXq8,))
                for g in range(2):
                    for i in range(8):
                        ic = 8 * g + i
                        P.op("tensor", lambda e, ic=ic, i=i: e.transpose(
                            bass.AP(p8, 256 * i, [[2048, 128], [2, 128]]),
                            xq8[:, 128 * ic:128 * ic + 128], id8[:, :]),
                            reads=(rXq8, rConst), writes=(rP8,))
                    P.op("vector", lambda e, g=g, tc=tc: e.tensor_copy(
                        bass.AP(xq8T, 2048 * 8 * g + 128 * tc, [[NC * S, 128], [2048, 8], [1, 128]]),
                        bass.AP(p8, 0, [[2048, 128], [256, 8], [2, 128]])),
                        reads=(rP8,), writes=(rXq8T,))
            pby = tc % 2
            pbf = 2 + (tc % 2)
            P.op("gpsimd", dma("gpsimd", scrF[:, 512 * (tc % 2):512 * (tc % 2) + 512],
                               bass.AP(dX, tc * 128 * D + 512 * oc, [[D, 128], [1, 512]])),
                 writes=(rScrF,), inc=16)
            P.op("tensor", lambda e, oc=oc, pby=pby: e.matmul(
                pb[pby][:, :], onesF[96:97, 0:128], barena[96:97, 512 * oc:512 * oc + 512],
                start=True, stop=False, tile_position=(96, 0)),
                reads=(rConst,), writes=(rPb[pby],))
            for dc in range(NC):
                P.op("tensor", lambda e, dc=dc, tc=tc, pg=pg, pby=pby: e.matmul(
                    pb[pby][:, :], aT[:, 2048 * dc + 128 * tc:2048 * dc + 128 * tc + 128],
                    WoB[:, NC * 512 * pg + 512 * dc:NC * 512 * pg + 512 * dc + 512],
                    start=False, stop=(dc == NC - 1)),
                    reads=(rAT, rWoB[pg]), writes=(rPb[pby],))
            for ic in range(NC):
                P.op("tensor", lambda e, ic=ic, tc=tc, oc=oc, pbf=pbf: e.matmul(
                    pb[pbf][:, :], xq8T[:, 2048 * ic + 128 * tc:2048 * ic + 128 * tc + 128],
                    w8T[:, 2048 * ic + 512 * oc:2048 * ic + 512 * oc + 512],
                    start=(ic == 0), stop=(ic == NC - 1)),
                    reads=(rXq8T, rW8T), writes=(rPb[pbf],))
            ts = 1024 + 512 * (tc % 2)
            P.op("vector", lambda e, tc=tc, pbf=pbf, ts=ts: e.scalar_tensor_tensor(
                scrF[:, ts:ts + 512], pb[pbf][:, :], ffn_scale,
                scrF[:, 512 * (tc % 2):512 * (tc % 2) + 512],
                ALU.mult, ALU.add),
                reads=(rPb[pbf],), writes=(rScrF,))
            P.op("vector", lambda e, pby=pby, ts=ts: e.tensor_tensor(
                scrF[:, ts:ts + 512], scrF[:, ts:ts + 512],
                pb[pby][:, :], ALU.add),
                reads=(rPb[pby],), writes=(rScrF,))
            P.op("gpsimd", lambda e, tc=tc, oc=oc, ts=ts: e.dma_start(
                bass.AP(dY, tc * 128 * D + 512 * oc, [[D, 128], [1, 512]]),
                scrF[:, ts:ts + 512]),
                reads=(rScrF,), writes=(rY,), inc=16)

    dbg("xq8T", xq8T[:, :], [128, NC * S], FP8, rXq8T)
    dbg("w8T", w8T[:, :], [128, NC * D], FP8, rW8T)

    P._wait("gpsimd", "gpsimd", P.cnt["gpsimd"])

    with nc.Block() as block:
        @block.sync
        def _(e):
            for f in P.ops["sync"]:
                f(e)

        @block.gpsimd
        def _(e):
            for f in P.ops["gpsimd"]:
                f(e)

        @block.tensor
        def _(e):
            for f in P.ops["tensor"]:
                f(e)

        @block.vector
        def _(e):
            for f in P.ops["vector"]:
                f(e)

        @block.scalar
        def _(e):
            for f in P.ops["scalar"]:
                f(e)

    return nc, es


def kernel(L0_attn_x, L0_attn_Wqkv, L0_attn_b_qkv, L0_attn_Wout, L0_attn_b_out,
           L0_ffn_weight, L0_ffn_weight_scale, L0_ffn_input_scale):
    x = np.ascontiguousarray(np.asarray(L0_attn_x), dtype=np.float32)
    wqkv = np.ascontiguousarray(np.asarray(L0_attn_Wqkv), dtype=np.float32)
    wout = np.ascontiguousarray(np.asarray(L0_attn_Wout), dtype=np.float32)
    w8 = np.asarray(L0_ffn_weight)
    if w8.dtype != f8dt:
        w8 = w8.view(f8dt) if w8.dtype.itemsize == 1 else w8.astype(f8dt)
    w8 = np.ascontiguousarray(w8)
    bq = np.asarray(L0_attn_b_qkv, dtype=np.float32)
    bo = np.asarray(L0_attn_b_out, dtype=np.float32)

    barena = np.zeros((128, D), dtype=np.float32)
    barena[0] = bq[0:D]
    barena[32] = bq[D:2 * D]
    barena[64] = bq[2 * D:3 * D]
    barena[96] = bo

    ctab, stab, tri, idf, idb, id8 = _host_consts()

    s_in = float(np.asarray(L0_ffn_input_scale, dtype=np.float32))
    s_w = float(np.asarray(L0_ffn_weight_scale, dtype=np.float32))
    nc, es = build_nc(qdq_mul=float(np.float32(1.0) / np.float32(s_in)),
                      ffn_scale=float(np.float32(s_w) * np.float32(s_in)))
    shared = {
        "wqkv": wqkv, "wout": wout, "w8": w8, "barena": barena,
        "ctab": ctab, "stab": stab, "tri": tri,
        "identf": idf, "identb": idb, "ident8": id8,
    }
    in_maps = [dict(shared, x=np.ascontiguousarray(x[c])) for c in range(8)]
    res = run_bass_kernel_spmd(nc, in_maps, core_ids=list(range(8)))
    es.close()
    out = np.stack([np.asarray(r["y"], dtype=np.float32) for r in res.results], axis=0)
    return out


# version 08
# speedup vs baseline: 1.4887x; 1.0660x over previous
import numpy as np
import ml_dtypes
import concourse.bass as bass
import concourse.mybir as mybir
from concourse.bass_utils import run_bass_kernel_spmd

F32 = mybir.dt.float32
BF16 = mybir.dt.bfloat16
FP8 = mybir.dt.float8e4
AF = mybir.ActivationFunctionType
ALU = mybir.AluOpType
AX = mybir.AxisListType

S = 2048
D = 2048
NH = 32
HD = 64
ROT = 32
NPAIR = 16
NC = 16
bfdt = ml_dtypes.bfloat16
f8dt = ml_dtypes.float8_e4m3fn


def _host_consts():
    t = np.arange(S, dtype=np.float64)
    inv = 10000.0 ** (-np.arange(0, ROT, 2, dtype=np.float64) / ROT)
    fr = np.outer(t, inv)
    cos = np.cos(fr)
    sin = np.sin(fr)
    ctab = np.zeros((128, S), dtype=np.float64)
    stab = np.zeros((128, S), dtype=np.float64)
    for r in range(128):
        hb = r % 64
        if hb < 16:
            ctab[r] = cos[:, hb]
            stab[r] = sin[:, hb]
        elif hb < 32:
            ctab[r] = cos[:, hb - 16]
            stab[r] = -sin[:, hb - 16]
        else:
            ctab[r] = 1.0
            stab[r] = 0.0
    tri = (np.arange(128)[None, :] >= np.arange(128)[:, None])
    return (ctab.astype(bfdt), stab.astype(bfdt), tri.astype(bfdt),
            np.eye(128, dtype=np.float32), np.eye(128, dtype=bfdt),
            np.eye(128, dtype=f8dt))


class Prog:
    ENGS = ("sync", "gpsimd", "tensor", "vector", "scalar")

    def __init__(self, nc, es):
        self.nc = nc
        self.ops = {e: [] for e in self.ENGS}
        self.sem = {}
        self.cnt = {}
        self.satisfied = {e: {} for e in self.ENGS}
        for e in self.ENGS:
            self.sem[e] = es.enter_context(nc.semaphore("sem_" + e))
            self.cnt[e] = 0

    def _wait(self, eng, prod, val):
        if val <= 0:
            return
        if self.satisfied[eng].get(prod, 0) >= val:
            return
        self.satisfied[eng][prod] = val
        sem = self.sem[prod]
        self.ops[eng].append(lambda e, s=sem, v=val: e.wait_ge(s, v))

    def op(self, eng, fn, reads=(), writes=(), inc=1, small=False):
        deps = {}
        need_drain = False
        for r in reads:
            for p, v in r.w.items():
                deps[p] = max(deps.get(p, 0), v)
            if r.w_eng == eng and (r.w_small or small):
                need_drain = True
        for r in writes:
            for p, v in r.w.items():
                deps[p] = max(deps.get(p, 0), v)
            for p, v in r.r.items():
                deps[p] = max(deps.get(p, 0), v)
        for p, v in deps.items():
            if p != eng:
                self._wait(eng, p, v)
        if need_drain:
            self.ops[eng].append(lambda e: e.drain())
        self.cnt[eng] += inc
        n = self.cnt[eng]
        sem = self.sem[eng]
        if inc:
            self.ops[eng].append(lambda e, f=fn, s=sem, a=inc: f(e).then_inc(s, a))
        else:
            self.ops[eng].append(fn)
        for r in writes:
            r.w = {eng: n}
            r.r = {}
            r.w_eng = eng
            r.w_small = small
        for r in reads:
            r.r[eng] = n
        return n

    def barrier(self):
        snap = dict(self.cnt)
        for e in self.ENGS:
            for p, v in snap.items():
                if p != e and v > 0:
                    self._wait(e, p, v)


class Res:
    __slots__ = ("w", "r", "w_eng", "w_small")

    def __init__(self):
        self.w = {}
        self.r = {}
        self.w_eng = None
        self.w_small = False


def build_nc(qdq_mul=4.0 / 3.0, ffn_scale=1.125, debug=False):
    nc = bass.Bass("TRN2", target_bir_lowering=False)
    from contextlib import ExitStack
    es = ExitStack()

    dX = nc.dram_tensor("x", [S, D], F32, kind="ExternalInput")
    dWqkv = nc.dram_tensor("wqkv", [D, 3 * D], F32, kind="ExternalInput")
    dWout = nc.dram_tensor("wout", [D, D], F32, kind="ExternalInput")
    dW8 = nc.dram_tensor("w8", [D, D], FP8, kind="ExternalInput")
    dBar = nc.dram_tensor("barena", [128, D], F32, kind="ExternalInput")
    dCtab = nc.dram_tensor("ctab", [128, S], BF16, kind="ExternalInput")
    dStab = nc.dram_tensor("stab", [128, S], BF16, kind="ExternalInput")
    dTri = nc.dram_tensor("tri", [128, 128], BF16, kind="ExternalInput")
    dIf = nc.dram_tensor("identf", [128, 128], F32, kind="ExternalInput")
    dIb = nc.dram_tensor("identb", [128, 128], BF16, kind="ExternalInput")
    dI8 = nc.dram_tensor("ident8", [128, 128], FP8, kind="ExternalInput")
    dY = nc.dram_tensor("y", [S, D], F32, kind="ExternalOutput")

    ps = lambda n, sh, dt: es.enter_context(nc.psum_tensor(n, sh, dt))

    ARENA = 207 * 1024
    arena = es.enter_context(nc.sbuf_tensor("sb_arena", [128, ARENA], mybir.dt.uint8))
    base = nc.lookup_mloc(arena).addr
    cur = [0]

    def _sz(sh, dt):
        n = 1
        for d in sh[1:]:
            n *= d
        return (n * mybir.dt.size(dt) + 127) & ~127

    def sb(name, sh, dt, at=None):
        off = cur[0] if at is None else at
        if at is None:
            cur[0] += _sz(sh, dt)
            assert cur[0] <= ARENA, (name, cur[0])
        return nc.alloc_sbuf_tensor_at(name, sh, dt, offset=base + off)

    _x0 = cur[0]
    xnT = sb("xnT", [128, NC * S], BF16)
    w8T = sb("w8T", [128, NC * D], FP8, at=_x0)
    WoB = sb("WoB", [128, 2 * NC * 512], BF16, at=_x0 + NC * D)

    _y0 = cur[0]
    qT = sb("qT", [128, S], BF16)
    kT = sb("kT", [128, S], BF16)
    vA = sb("vA", [128, NC * HD], BF16)
    vB = sb("vB", [128, NC * HD], BF16)
    Wq = sb("Wq", [128, NC * 128], BF16)
    Wk = sb("Wk", [128, NC * 128], BF16)
    Wv = sb("Wv", [128, NC * 128], BF16)
    probs = sb("probs", [128, 4 * 512], BF16)
    ctab = sb("ctab", [128, S], BF16)
    stab = sb("stab", [128, S], BF16)
    xq8T = sb("xq8T", [128, NC * S], FP8, at=_y0)
    xq8 = sb("xq8", [128, D], FP8, at=_y0 + NC * S)
    xt2 = sb("xt2", [128, 2048], F32, at=_y0)

    aT = sb("aT", [128, NC * S], BF16)
    Wst = sb("Wst", [128, 2048], F32)
    tri = sb("tri", [128, 128], BF16)
    idf = sb("idf", [128, 128], F32)
    idb = sb("idb", [128, 128], BF16)
    id8 = sb("id8", [128, 128], FP8)
    barena = sb("barena", [128, D], F32)
    onesF = sb("onesF", [128, 512], F32)
    onesB = sb("onesB", [128, 16], BF16)
    xt = sb("xt", [128, 2048], F32)
    scrF = sb("scrF", [128, 2560], F32)
    sRec = sb("sRec", [128, 512], F32)
    miscF = sb("miscF", [128, 16], F32)
    negc = sb("negc", [128, 1], F32)
    lnT = sb("lnT", [128, 24], F32)
    epsT = sb("epsT", [128, 1], F32)
    w8st = sb("w8st", [128, D], FP8)
    assert cur[0] <= ARENA, cur[0]
    if debug:
        print("SBUF arena used:", cur[0], "of", ARENA)

    pb = [ps(f"pb{i}", [128, 512], F32) for i in range(7)]
    p8 = ps("p8", [128, 2048], FP8)

    P = Prog(nc, es)

    R = lambda: Res()
    rXt, rWst, rWq, rWk, rWv = R(), R(), R(), R(), R()
    rQT, rKT, rVA, rVB = R(), R(), R(), R()
    rXnT, rAT = R(), R()
    rPb = [R() for _ in range(7)]
    rP8 = R()
    rProbs = [R() for _ in range(4)]
    rScrF = R()
    rSRec, rMisc, rNegc = R(), R(), R()
    rSRec2 = R()
    rLn = R()
    rW8T, rXq8, rXq8T, rWoB = R(), R(), R(), [R(), R()]
    rXt2, rLn2, rW8st = R(), R(), R()
    rConst = R()
    rY = R()

    def dma(eng, dst, src):
        return lambda e, d=dst, s=src: e.dma_start(d, s)

    for dst, src in ((ctab[:, :], dCtab.ap()), (stab[:, :], dStab.ap()),
                     (tri[:, :], dTri.ap()), (idf[:, :], dIf.ap()),
                     (idb[:, :], dIb.ap()), (id8[:, :], dI8.ap()),
                     (barena[:, :], dBar.ap())):
        P.op("sync", dma("sync", dst, src), writes=(rConst,), inc=16)
    P.op("vector", lambda e: e.memset(onesF[:, :], 1.0), writes=(rConst,))
    P.op("vector", lambda e: e.memset(onesB[:, :], 1.0), writes=(rConst,))
    P.op("vector", lambda e: e.memset(epsT[:, :], 1e-5), writes=(rConst,))

    def emit_ln(buf=None, rB=None, c0=0, rL=None):
        buf = xt if buf is None else buf
        rB = rXt if rB is None else rB
        rL = rLn if rL is None else rL
        P.op("vector", lambda e, c0=c0, buf=buf: e.reduce_sum(lnT[:, c0:c0 + 1], buf[:, :], AX.X),
             reads=(rB,), writes=(rL,), small=True)
        P.op("scalar", lambda e, c0=c0: e.mul(lnT[:, c0 + 1:c0 + 2], lnT[:, c0:c0 + 1], 1.0 / D),
             reads=(rL,), writes=(rL,), small=True)
        P.op("scalar", lambda e, buf=buf: e.square(scrF[:, 0:2048], buf[:, :]),
             reads=(rB,), writes=(rScrF,))
        P.op("vector", lambda e, c0=c0: e.reduce_sum(lnT[:, c0 + 2:c0 + 3], scrF[:, 0:2048], AX.X),
             reads=(rScrF,), writes=(rL,), small=True)
        P.op("scalar", lambda e, c0=c0: e.mul(lnT[:, c0 + 3:c0 + 4], lnT[:, c0 + 2:c0 + 3], 1.0 / D),
             reads=(rL,), writes=(rL,), small=True)
        P.op("vector", lambda e, c0=c0: e.tensor_tensor(lnT[:, c0 + 4:c0 + 5], lnT[:, c0 + 1:c0 + 2],
                                                 lnT[:, c0 + 1:c0 + 2], ALU.mult),
             reads=(rL,), writes=(rL,), small=True)
        P.op("vector", lambda e, c0=c0: e.tensor_tensor(lnT[:, c0 + 5:c0 + 6], lnT[:, c0 + 3:c0 + 4],
                                                 lnT[:, c0 + 4:c0 + 5], ALU.subtract),
             reads=(rL,), writes=(rL,), small=True)
        P.op("vector", lambda e, c0=c0: e.tensor_scalar(lnT[:, c0 + 6:c0 + 7], lnT[:, c0 + 5:c0 + 6],
                                                 epsT[:, 0:1], None, ALU.add),
             reads=(rConst, rL), writes=(rL,), small=True)
        P.op("vector", lambda e, c0=c0: e.reciprocal(lnT[:, c0 + 7:c0 + 8], lnT[:, c0 + 6:c0 + 7]),
             reads=(rL,), writes=(rL,), small=True)
        P.op("scalar", lambda e, c0=c0: e.sqrt(lnT[:, c0 + 8:c0 + 9], lnT[:, c0 + 7:c0 + 8]),
             reads=(rL,), writes=(rL,), small=True)

    for tc in range(NC):
        buf, rB = (xt, rXt) if tc % 2 == 0 else (xt2, rXt2)
        c0 = 12 * (tc % 2)
        rL = rLn if tc % 2 == 0 else rLn2
        P.op("sync", dma("sync", buf[:, :],
                         bass.AP(dX, tc * 128 * D, [[D, 128], [1, D]])),
             writes=(rB,), inc=16)
        emit_ln(buf, rB, c0, rL)
        P.op("vector", lambda e, buf=buf, c0=c0: e.tensor_scalar(
            buf[:, :], buf[:, :], lnT[:, c0 + 1:c0 + 2],
            lnT[:, c0 + 8:c0 + 9], ALU.subtract, ALU.mult),
            reads=(rL,), writes=(rB,))
        for i in range(NC):
            b = i // 4
            P.op("tensor", lambda e, i=i, b=b, buf=buf: e.transpose(
                pb[b][:, 128 * (i % 4):128 * (i % 4) + 128], buf[:, 128 * i:128 * i + 128],
                idf[:, :]),
                reads=(rB, rConst), writes=(rPb[b],))
        for b in range(4):
            P.op("vector", lambda e, b=b, tc=tc: e.tensor_copy(
                bass.AP(xnT, 8192 * b + 128 * tc, [[NC * S, 128], [2048, 4], [1, 128]]),
                pb[b][:, :]),
                reads=(rPb[b],), writes=(rXnT,))

    dbg_out = {}

    def dbg(name, src_ap, shape, dt, res):
        if not debug:
            return
        d = nc.dram_tensor("dbg_" + name, shape, dt, kind="ExternalOutput")
        dbg_out[name] = d
        P.op("gpsimd", lambda e: e.dma_start(d.ap(), src_ap), reads=(res,), inc=16)

    dbg("xnT", xnT[:, :], [128, NC * S], BF16, rXnT)

    def w8T_batches():
        firstref = [True]
        def dma_chunk(oc8):
            P.op("gpsimd", dma("gpsimd", w8st[:, :],
                               bass.AP(dW8, oc8 * 128 * D, [[D, 128], [1, D]])),
                 writes=(rW8st,), inc=16)
        def batch(oc8, g):
            for i in range(8):
                ic = 8 * g + i
                P.op("tensor", lambda e, ic=ic, i=i: e.transpose(
                    bass.AP(p8, 256 * i, [[2048, 128], [2, 128]]),
                    w8st[:, 128 * ic:128 * ic + 128], id8[:, :]),
                    reads=(rW8st, rConst), writes=(rP8,))
            wr = (rW8T, rXnT) if firstref[0] else (rW8T,)
            firstref[0] = False
            P.op("vector", lambda e, g=g, oc8=oc8: e.tensor_copy(
                bass.AP(w8T, 2048 * 8 * g + 128 * oc8, [[NC * D, 128], [2048, 8], [1, 128]]),
                bass.AP(p8, 0, [[2048, 128], [256, 8], [2, 128]])),
                reads=(rP8,), writes=wr)
        out = []
        for oc8 in range(NC):
            out.append(lambda oc8=oc8: dma_chunk(oc8))
            for g in range(2):
                out.append(lambda oc8=oc8, g=g: batch(oc8, g))
        return out

    def emit_wout_load(oc):
        pg = oc % 2
        for sub in range(4):
            o0 = 512 * oc + 128 * sub
            P.op("sync", dma("sync", bass.AP(Wst, 0, [[2048, 128], [128, 16], [1, 128]]),
                             bass.AP(dWout, o0, [[D, 128], [128 * D, 16], [1, 128]])),
                 writes=(rWst,), inc=16)
            wr = (rWoB[pg], rXnT) if oc == 0 and sub == 0 else (rWoB[pg],)
            P.op("vector", lambda e, pg=pg, sub=sub: e.tensor_copy(
                bass.AP(WoB, NC * 512 * pg + 128 * sub, [[2 * NC * 512, 128], [512, 16], [1, 128]]),
                Wst[:, :]),
                reads=(rWst,), writes=wr)

    for p in range(NPAIR):
        for bi, (c0, rW) in enumerate(((128 * p, rWq), (D + 128 * p, rWk),
                                       (2 * D + 128 * p, rWv))):
            P.op("sync", dma("sync", bass.AP(Wst, 0, [[2048, 128], [128, 16], [1, 128]]),
                             bass.AP(dWqkv, c0, [[3 * D, 128], [128 * 3 * D, 16], [1, 128]])),
                 writes=(rWst,), inc=16)
            dstW = (Wq, Wk, Wv)[bi]
            P.op("vector", lambda e, d=dstW: e.tensor_copy(d[:, :], Wst[:, :]),
                 reads=(rWst,), writes=(rW,))

        for mi, (Wm, rWm, brow, dstT, rDst) in enumerate(
                ((Wq, rWq, 0, qT, rQT), (Wk, rWk, 32, kT, rKT))):
            for tc4 in range(4):
                bank = (mi * 4 + tc4) % 4 if False else tc4
                P.op("tensor", lambda e, brow=brow, p=p, bank=bank: e.matmul(
                    pb[bank][:, :], barena[brow:brow + 1, 128 * p:128 * p + 128],
                    onesF[brow:brow + 1, 0:512], start=True, stop=False),
                    reads=(rConst,), writes=(rPb[bank],))
                for dc in range(NC):
                    P.op("tensor", lambda e, Wm=Wm, dc=dc, tc4=tc4, bank=bank: e.matmul(
                        pb[bank][:, :], Wm[:, 128 * dc:128 * dc + 128],
                        xnT[:, 2048 * dc + 512 * tc4:2048 * dc + 512 * tc4 + 512],
                        start=False, stop=(dc == NC - 1)),
                        reads=(rWm, rXnT), writes=(rPb[bank],))
                cs = slice(512 * tc4, 512 * tc4 + 512)
                P.op("vector", lambda e, bank=bank, cs=cs: e.tensor_tensor(
                    scrF[:, 0:512], pb[bank][:, :], ctab[:, cs], ALU.mult),
                    reads=(rPb[bank], rConst), writes=(rScrF,))
                P.op("vector", lambda e, bank=bank, cs=cs: e.tensor_tensor(
                    scrF[:, 512:1024], pb[bank][:, :], stab[:, cs], ALU.mult),
                    reads=(rPb[bank], rConst), writes=(rScrF,))
                P.op("vector", lambda e: e.stream_shuffle(
                    scrF[:, 1024:1536], scrF[:, 512:1024],
                    [(i + 16) % 32 for i in range(32)]),
                    reads=(rScrF,), writes=(rScrF,))
                P.op("vector", lambda e, dstT=dstT, cs=cs: e.tensor_tensor(
                    dstT[:, cs], scrF[:, 0:512], scrF[:, 1024:1536], ALU.add),
                    reads=(rScrF,), writes=(rDst,))

        P.op("scalar", lambda e: e.square(probs[:, 0:2048], qT[:, :]),
             reads=(rQT,), writes=(rProbs[0], rProbs[1], rProbs[2], rProbs[3]))
        for i in range(4):
            P.op("tensor", lambda e, i=i: e.matmul(
                pb[i][0:1, :], onesB[:, 0:1], probs[:, 512 * i:512 * i + 512],
                start=True, stop=True),
                reads=(rProbs[i], rConst), writes=(rPb[i],))
        for i in range(4):
            P.op("vector", lambda e, i=i: e.reduce_max(
                miscF[0:1, i:i + 1], pb[i][0:1, :], AX.X),
                reads=(rPb[i],), writes=(rMisc,), small=True)
        P.op("scalar", lambda e: e.square(probs[:, 0:2048], kT[:, :]),
             reads=(rKT,), writes=(rProbs[0], rProbs[1], rProbs[2], rProbs[3]))
        for i in range(4):
            P.op("tensor", lambda e, i=i: e.matmul(
                pb[i][0:1, :], onesB[:, 0:1], probs[:, 512 * i:512 * i + 512],
                start=True, stop=True),
                reads=(rProbs[i], rConst), writes=(rPb[i],))
        for i in range(4):
            P.op("vector", lambda e, i=i: e.reduce_max(
                miscF[0:1, 4 + i:5 + i], pb[i][0:1, :], AX.X),
                reads=(rPb[i],), writes=(rMisc,), small=True)
        P.op("vector", lambda e: e.reduce_max(miscF[0:1, 8:9], miscF[0:1, 0:8], AX.X),
             reads=(rMisc,), writes=(rMisc,), small=True)
        P.op("scalar", lambda e: e.mul(miscF[0:1, 9:10], miscF[0:1, 8:9], -0.125),
             reads=(rMisc,), writes=(rMisc,), small=True)
        P.op("tensor", lambda e: e.matmul(pb[6][:, 0:1], onesF[0:1, 0:128],
                                          miscF[0:1, 9:10], start=True, stop=True),
             reads=(rMisc, rConst), writes=(rPb[6],))
        P.op("vector", lambda e: e.tensor_copy(negc[:, :], pb[6][:, 0:1]),
             reads=(rPb[6],), writes=(rNegc,), small=True)

        pend_tr = []

        def flush_vtr():
            for fn in pend_tr:
                fn()
            del pend_tr[:]

        for tc4 in range(4):
            bank = tc4
            P.op("tensor", lambda e, p=p, bank=bank: e.matmul(
                pb[bank][:, :], barena[64:65, 128 * p:128 * p + 128],
                onesF[64:65, 0:512], start=True, stop=False),
                reads=(rConst,), writes=(rPb[bank],))
            for dc in range(NC):
                P.op("tensor", lambda e, dc=dc, tc4=tc4, bank=bank: e.matmul(
                    pb[bank][:, :], Wv[:, 128 * dc:128 * dc + 128],
                    xnT[:, 2048 * dc + 512 * tc4:2048 * dc + 512 * tc4 + 512],
                    start=False, stop=(dc == NC - 1)),
                    reads=(rWv, rXnT), writes=(rPb[bank],))
            sl = 512 * (tc4 % 2)
            P.op("vector", lambda e, bank=bank, sl=sl: e.tensor_copy(
                scrF[:, sl:sl + 512], pb[bank][:, :]),
                reads=(rPb[bank],), writes=(rScrF,))
            tb = 4 + (tc4 % 2)

            def emit_tr(tc4=tc4, sl=sl, tb=tb):
                for i in range(4):
                    P.op("tensor", lambda e, i=i, sl=sl, tb=tb: e.transpose(
                        pb[tb][:, 128 * i:128 * i + 128],
                        scrF[:, sl + 128 * i:sl + 128 * i + 128], idf[:, :]),
                        reads=(rScrF, rConst), writes=(rPb[tb],))
                for i in range(4):
                    sc = tc4 * 4 + i
                    P.op("vector", lambda e, i=i, sc=sc, tb=tb: e.tensor_copy(
                        vA[:, 64 * sc:64 * sc + 64], pb[tb][:, 128 * i:128 * i + 64]),
                        reads=(rPb[tb],), writes=(rVA,))
                    P.op("vector", lambda e, i=i, sc=sc, tb=tb: e.tensor_copy(
                        vB[:, 64 * sc:64 * sc + 64], pb[tb][:, 128 * i + 64:128 * i + 128]),
                        reads=(rPb[tb],), writes=(rVB,))
            flush_vtr()
            pend_tr.append(emit_tr)
        flush_vtr()

        if p == 0:
            dbg("qT0", qT[:, :], [128, S], BF16, rQT)
            dbg("kT0", kT[:, :], [128, S], BF16, rKT)
            dbg("negc0", negc[:, :], [128, 1], F32, rNegc)
            dbg("vA0", vA[:, :], [128, NC * HD], BF16, rVA)
            dbg("vB0", vB[:, :], [128, NC * HD], BF16, rVB)

        w8w = w8T_batches() if p == NPAIR - 1 else []
        if p == NPAIR - 1:
            emit_wout_load(0)

        def jlist_of(T):
            return [4 * T] + list(range(0, 4 * T)) + [4 * T + r for r in (1, 2, 3)]

        def emit_scores(T, idx):
            j = jlist_of(T)[idx]
            par = idx % 2
            bA, bB = par, 2 + par
            r0 = max(0, 128 * (j - 4 * T))
            cs = slice(r0, 512)
            P.op("tensor", lambda e, j=j, T=T, cs=cs, bA=bA: e.matmul(
                pb[bA][:, cs], kT[0:64, 128 * j:128 * j + 128],
                qT[0:64, 512 * T + cs.start:512 * T + 512],
                start=True, stop=True, tile_position=(0, 0)),
                reads=(rKT, rQT), writes=(rPb[bA],))
            P.op("tensor", lambda e, j=j, T=T, cs=cs, bB=bB: e.matmul(
                pb[bB][:, cs], kT[64:128, 128 * j:128 * j + 128],
                qT[64:128, 512 * T + cs.start:512 * T + 512],
                start=True, stop=True, tile_position=(64, 0)),
                reads=(rKT, rQT), writes=(rPb[bB],))

        def emit_expmask(T, idx):
            j = jlist_of(T)[idx]
            par = idx % 2
            bA, bB = par, 2 + par
            r0 = max(0, 128 * (j - 4 * T))
            cs = slice(r0, 512)
            pApr, pBpr = rProbs[par], rProbs[2 + par]
            P.op("scalar", lambda e, cs=cs, bA=bA, par=par: e.activation(
                probs[:, 512 * par + cs.start:512 * par + 512], pb[bA][:, cs],
                AF.Exp, bias=negc[:, 0:1], scale=0.125),
                reads=(rPb[bA], rNegc), writes=(pApr,))
            P.op("scalar", lambda e, cs=cs, bB=bB, par=par: e.activation(
                probs[:, 1024 + 512 * par + cs.start:1024 + 512 * par + 512],
                pb[bB][:, cs], AF.Exp, bias=negc[:, 0:1], scale=0.125),
                reads=(rPb[bB], rNegc), writes=(pBpr,))
            if j >= 4 * T:
                m0 = r0
                P.op("vector", lambda e, m0=m0, par=par: e.tensor_tensor(
                    probs[:, 512 * par + m0:512 * par + m0 + 128],
                    probs[:, 512 * par + m0:512 * par + m0 + 128],
                    tri[:, :], ALU.mult),
                    reads=(rConst,), writes=(pApr,))
                P.op("vector", lambda e, m0=m0, par=par: e.tensor_tensor(
                    probs[:, 1024 + 512 * par + m0:1024 + 512 * par + m0 + 128],
                    probs[:, 1024 + 512 * par + m0:1024 + 512 * par + m0 + 128],
                    tri[:, :], ALU.mult),
                    reads=(rConst,), writes=(pBpr,))

        def emit_denav(T, idx):
            j = jlist_of(T)[idx]
            par = idx % 2
            r0 = max(0, 128 * (j - 4 * T))
            cs = slice(r0, 512)
            first = (idx == 0)
            dr = 64 * (T % 2)
            pApr, pBpr = rProbs[par], rProbs[2 + par]
            P.op("tensor", lambda e, cs=cs, par=par, first=first, dr=dr: e.matmul(
                pb[5][dr:dr + 1, cs], onesB[:, 0:1],
                probs[:, 512 * par + cs.start:512 * par + 512],
                start=first, stop=False, tile_position=(0, dr)),
                reads=(pApr, rConst), writes=(rPb[5],))
            P.op("tensor", lambda e, cs=cs, par=par, first=first, dr=dr: e.matmul(
                pb[5][dr + 32:dr + 33, cs], onesB[:, 0:1],
                probs[:, 1024 + 512 * par + cs.start:1024 + 512 * par + 512],
                start=first, stop=False, tile_position=(0, dr + 32)),
                reads=(pBpr, rConst), writes=(rPb[5],))
            P.op("tensor", lambda e, j=j, cs=cs, par=par, first=first: e.matmul(
                pb[4][0:64, cs], vA[:, 64 * j:64 * j + 64],
                probs[:, 512 * par + cs.start:512 * par + 512],
                start=first, stop=False, tile_position=(0, 0)),
                reads=(pApr, rVA), writes=(rPb[4],))
            P.op("tensor", lambda e, j=j, cs=cs, par=par, first=first: e.matmul(
                pb[4][64:128, cs], vB[:, 64 * j:64 * j + 64],
                probs[:, 1024 + 512 * par + cs.start:1024 + 512 * par + 512],
                start=first, stop=False, tile_position=(0, 64)),
                reads=(pBpr, rVB), writes=(rPb[4],))

        def emit_epi_pre(T):
            sl = 1536 + 512 * (T % 2)
            dr = 64 * (T % 2)
            P.op("scalar", lambda e, sl=sl: e.copy(scrF[:, sl:sl + 512],
                                                   pb[4][:, :]),
                 reads=(rPb[4],), writes=(rScrF,))
            sr = 64 * (T % 2)
            rS = rSRec if T % 2 == 0 else rSRec2
            P.op("vector", lambda e, dr=dr, sr=sr: e.reciprocal(sRec[sr:sr + 33, :],
                                                         pb[5][dr:dr + 33, :]),
                 reads=(rPb[5],), writes=(rS,), small=True)

        def emit_epi_bcast(T, p):
            sr = 64 * (T % 2)
            rS = rSRec if T % 2 == 0 else rSRec2
            P.op("tensor", lambda e, sr=sr: e.matmul(
                pb[6][0:64, :], onesF[sr:sr + 1, 0:64],
                sRec[sr:sr + 1, :], start=True, stop=True,
                tile_position=(sr, 0)),
                reads=(rS, rConst), writes=(rPb[6],))
            P.op("tensor", lambda e, sr=sr: e.matmul(
                pb[6][64:128, :], onesF[sr + 32:sr + 33, 0:64],
                sRec[sr + 32:sr + 33, :], start=True, stop=True,
                tile_position=(sr + 32, 64)),
                reads=(rS, rConst), writes=(rPb[6],))
            sl = 1536 + 512 * (T % 2)
            P.op("vector", lambda e, p=p, T=T, sl=sl: e.tensor_tensor(
                aT[:, 2048 * p + 512 * T:2048 * p + 512 * T + 512],
                scrF[:, sl:sl + 512], pb[6][:, :], ALU.mult),
                reads=(rScrF, rPb[6]), writes=(rAT,))

        sched = [(T, idx) for T in range(4) for idx in range(len(jlist_of(T)))]
        emit_scores(*sched[0])
        pos = 0
        pend = []
        for T in range(4):
            for idx in range(len(jlist_of(T))):
                emit_expmask(T, idx)
                if pos + 1 < len(sched):
                    emit_scores(*sched[pos + 1])
                if pend and idx == 2:
                    pend.pop(0)()
                emit_denav(T, idx)
                if w8w:
                    w8w.pop(0)()
                pos += 1
            emit_epi_pre(T)
            while pend:
                pend.pop(0)()
            pend.append(lambda T=T, p=p: emit_epi_bcast(T, p))
        while pend:
            pend.pop(0)()
        while w8w:
            w8w.pop(0)()

    dbg("aT", aT[:, :], [128, NC * S], BF16, rAT)

    P.barrier()

    for oc in range(4):
        pg = oc % 2
        if oc + 1 < 4:
            emit_wout_load(oc + 1)
        for tc in range(NC):
            if oc == 0:
                P.op("sync", dma("sync", xt[:, :],
                                 bass.AP(dX, tc * 128 * D, [[D, 128], [1, D]])),
                     writes=(rXt,), inc=16)
                emit_ln()
                P.op("scalar", lambda e: e.mul(lnT[:, 9:10], lnT[:, 8:9], qdq_mul),
                     reads=(rLn,), writes=(rLn,), small=True)
                P.op("vector", lambda e: e.tensor_scalar(xq8[:, :], xt[:, :], lnT[:, 1:2],
                                                         lnT[:, 9:10], ALU.subtract, ALU.mult),
                     reads=(rXt, rLn), writes=(rXq8,))

                def xq_batch(g, tc=tc):
                    for i in range(8):
                        ic = 8 * g + i
                        P.op("tensor", lambda e, ic=ic, i=i: e.transpose(
                            bass.AP(p8, 256 * i, [[2048, 128], [2, 128]]),
                            xq8[:, 128 * ic:128 * ic + 128], id8[:, :]),
                            reads=(rXq8, rConst), writes=(rP8,))
                    P.op("vector", lambda e, g=g, tc=tc: e.tensor_copy(
                        bass.AP(xq8T, 2048 * 8 * g + 128 * tc, [[NC * S, 128], [2048, 8], [1, 128]]),
                        bass.AP(p8, 0, [[2048, 128], [256, 8], [2, 128]])),
                        reads=(rP8,), writes=(rXq8T,))
                xq_pend = [lambda: xq_batch(0), lambda: xq_batch(1)]
            else:
                xq_pend = []
            pby = tc % 2
            pbf = 2 + (tc % 2)
            P.op("gpsimd", dma("gpsimd", scrF[:, 512 * (tc % 2):512 * (tc % 2) + 512],
                               bass.AP(dX, tc * 128 * D + 512 * oc, [[D, 128], [1, 512]])),
                 writes=(rScrF,), inc=16)
            P.op("tensor", lambda e, oc=oc, pby=pby: e.matmul(
                pb[pby][:, :], onesF[96:97, 0:128], barena[96:97, 512 * oc:512 * oc + 512],
                start=True, stop=False, tile_position=(96, 0)),
                reads=(rConst,), writes=(rPb[pby],))
            for dc in range(NC):
                if xq_pend and dc in (5, 11):
                    xq_pend.pop(0)()
                P.op("tensor", lambda e, dc=dc, tc=tc, pg=pg, pby=pby: e.matmul(
                    pb[pby][:, :], aT[:, 2048 * dc + 128 * tc:2048 * dc + 128 * tc + 128],
                    WoB[:, NC * 512 * pg + 512 * dc:NC * 512 * pg + 512 * dc + 512],
                    start=False, stop=(dc == NC - 1)),
                    reads=(rAT, rWoB[pg]), writes=(rPb[pby],))
            for ic in range(NC):
                P.op("tensor", lambda e, ic=ic, tc=tc, oc=oc, pbf=pbf: e.matmul(
                    pb[pbf][:, :], xq8T[:, 2048 * ic + 128 * tc:2048 * ic + 128 * tc + 128],
                    w8T[:, 2048 * ic + 512 * oc:2048 * ic + 512 * oc + 512],
                    start=(ic == 0), stop=(ic == NC - 1)),
                    reads=(rXq8T, rW8T), writes=(rPb[pbf],))
            ts = 1024 + 512 * (tc % 2)
            P.op("vector", lambda e, tc=tc, pbf=pbf, ts=ts: e.scalar_tensor_tensor(
                scrF[:, ts:ts + 512], pb[pbf][:, :], ffn_scale,
                scrF[:, 512 * (tc % 2):512 * (tc % 2) + 512],
                ALU.mult, ALU.add),
                reads=(rPb[pbf],), writes=(rScrF,))
            P.op("vector", lambda e, pby=pby, ts=ts: e.tensor_tensor(
                scrF[:, ts:ts + 512], scrF[:, ts:ts + 512],
                pb[pby][:, :], ALU.add),
                reads=(rPb[pby],), writes=(rScrF,))
            P.op("gpsimd", lambda e, tc=tc, oc=oc, ts=ts: e.dma_start(
                bass.AP(dY, tc * 128 * D + 512 * oc, [[D, 128], [1, 512]]),
                scrF[:, ts:ts + 512]),
                reads=(rScrF,), writes=(rY,), inc=16)

    dbg("xq8T", xq8T[:, :], [128, NC * S], FP8, rXq8T)
    dbg("w8T", w8T[:, :], [128, NC * D], FP8, rW8T)

    P._wait("gpsimd", "gpsimd", P.cnt["gpsimd"])

    with nc.Block() as block:
        @block.sync
        def _(e):
            for f in P.ops["sync"]:
                f(e)

        @block.gpsimd
        def _(e):
            for f in P.ops["gpsimd"]:
                f(e)

        @block.tensor
        def _(e):
            for f in P.ops["tensor"]:
                f(e)

        @block.vector
        def _(e):
            for f in P.ops["vector"]:
                f(e)

        @block.scalar
        def _(e):
            for f in P.ops["scalar"]:
                f(e)

    return nc, es


def kernel(L0_attn_x, L0_attn_Wqkv, L0_attn_b_qkv, L0_attn_Wout, L0_attn_b_out,
           L0_ffn_weight, L0_ffn_weight_scale, L0_ffn_input_scale):
    x = np.ascontiguousarray(np.asarray(L0_attn_x), dtype=np.float32)
    wqkv = np.ascontiguousarray(np.asarray(L0_attn_Wqkv), dtype=np.float32)
    wout = np.ascontiguousarray(np.asarray(L0_attn_Wout), dtype=np.float32)
    w8 = np.asarray(L0_ffn_weight)
    if w8.dtype != f8dt:
        w8 = w8.view(f8dt) if w8.dtype.itemsize == 1 else w8.astype(f8dt)
    w8 = np.ascontiguousarray(w8)
    bq = np.asarray(L0_attn_b_qkv, dtype=np.float32)
    bo = np.asarray(L0_attn_b_out, dtype=np.float32)

    barena = np.zeros((128, D), dtype=np.float32)
    barena[0] = bq[0:D]
    barena[32] = bq[D:2 * D]
    barena[64] = bq[2 * D:3 * D]
    barena[96] = bo

    ctab, stab, tri, idf, idb, id8 = _host_consts()

    s_in = float(np.asarray(L0_ffn_input_scale, dtype=np.float32))
    s_w = float(np.asarray(L0_ffn_weight_scale, dtype=np.float32))
    nc, es = build_nc(qdq_mul=float(np.float32(1.0) / np.float32(s_in)),
                      ffn_scale=float(np.float32(s_w) * np.float32(s_in)))
    shared = {
        "wqkv": wqkv, "wout": wout, "w8": w8, "barena": barena,
        "ctab": ctab, "stab": stab, "tri": tri,
        "identf": idf, "identb": idb, "ident8": id8,
    }
    in_maps = [dict(shared, x=np.ascontiguousarray(x[c])) for c in range(8)]
    res = run_bass_kernel_spmd(nc, in_maps, core_ids=list(range(8)))
    es.close()
    out = np.stack([np.asarray(r["y"], dtype=np.float32) for r in res.results], axis=0)
    return out
